# version 17
# baseline (speedup 1.0000x reference)
import contextlib
import numpy as np
import concourse.bass as bass
import concourse.mybir as mybir
from concourse.bass_utils import run_bass_kernel_spmd

F32 = mybir.dt.float32
BF16 = mybir.dt.bfloat16
AF = mybir.ActivationFunctionType
ALU = mybir.AluOpType
AX = mybir.AxisListType

D = 2048
KC = 16
TPC = 1024
NT = 2
TN = 512
NE = 8
NIT = 10
FC = 11
HCN = 16
NCH = 32
NOWN = 8
EPS = 1e-5
SEQ = 4096
NB = 2
NEWTON = 4
NCORES = 8


def build_program():
    nc = bass.Bass("TRN2", target_bir_lowering=False)

    x_own = nc.dram_tensor("x_own", [128, KC, TPC], F32, kind="ExternalInput")
    x_stats = nc.dram_tensor("x_stats", [NCH, 128, D], F32, kind="ExternalInput")
    guw = [nc.dram_tensor(f"guw{e}", [2 * FC, 128, KC * 128], F32, kind="ExternalInput")
           for e in range(NE)]
    sguw = nc.dram_tensor("sguw", [4 * FC, 128, KC * 128], F32, kind="ExternalInput")
    dnw = [nc.dram_tensor(f"dnw{e}", [16, 128, FC * 128], F32, kind="ExternalInput")
           for e in range(NIT)]
    wvt = nc.dram_tensor("wvt", [KC * 2, 128, 1024], F32, kind="ExternalInput")
    wot = nc.dram_tensor("wot", [KC * 2, 128, 1024], F32, kind="ExternalInput")
    gwt = nc.dram_tensor("gwt", [128, KC * NE], F32, kind="ExternalInput")
    bvv = nc.dram_tensor("bvv", [128, KC], F32, kind="ExternalInput")
    bov = nc.dram_tensor("bov", [128, KC], F32, kind="ExternalInput")
    eyec = nc.dram_tensor("eyec", [128, 128], F32, kind="ExternalInput")
    onesc = nc.dram_tensor("onesc", [128, 1], F32, kind="ExternalInput")
    y_out = nc.dram_tensor("y_out", [128, KC, TPC], F32, kind="ExternalOutput")

    with contextlib.ExitStack() as ctx:
        E = ctx.enter_context

        y = E(nc.sbuf_tensor("y", [128, KC, TPC], F32))
        xnT = E(nc.sbuf_tensor("xnT", [128, KC, TPC], BF16))
        act = [E(nc.sbuf_tensor(f"act{i}", [128, FC, TPC], BF16)) for i in range(2)]
        gub = [E(nc.sbuf_tensor(f"gub{i}", [128, KC * 128], BF16)) for i in range(3)]
        dnb = [E(nc.sbuf_tensor(f"dnb{i}", [128, FC * 128], BF16)) for i in range(2)]
        stx = [E(nc.sbuf_tensor(f"stx{i}", [128, D], F32)) for i in range(2)]
        xn_tm = [E(nc.sbuf_tensor(f"xn_tm{i}", [128, D], BF16)) for i in range(2)]
        wvb = [E(nc.sbuf_tensor(f"wvb{i}", [128, 1024], BF16)) for i in range(2)]
        mbc = E(nc.sbuf_tensor("mbc", [128, TPC], BF16))
        rbc = E(nc.sbuf_tensor("rbc", [128, TPC], BF16))
        silu_s = E(nc.sbuf_tensor("silu_s", [128, TPC], F32))
        cbc = [E(nc.sbuf_tensor(f"cbc{i}", [128, TPC], BF16)) for i in range(2)]
        otile = [E(nc.sbuf_tensor(f"otile{i}", [128, TN], F32)) for i in range(2)]
        s_gw = E(nc.sbuf_tensor("s_gw", [128, KC * NE], F32))
        s_bv = E(nc.sbuf_tensor("s_bv", [128, KC], F32))
        s_bo = E(nc.sbuf_tensor("s_bo", [128, KC], F32))
        s_eye = E(nc.sbuf_tensor("s_eye", [128, 128], F32))
        s_onesf = E(nc.sbuf_tensor("s_onesf", [128, 1], F32))
        s_ones_bf = E(nc.sbuf_tensor("s_ones_bf", [128, 128], BF16))
        s_onec_bf = E(nc.sbuf_tensor("s_onec_bf", [128, 1], BF16))
        stB = E(nc.sbuf_tensor("stB", [128, 32], F32))
        oB = E(nc.sbuf_tensor("oB", [128, 96], F32))
        xaccs = E(nc.sbuf_tensor("xaccs", [128, 16], F32))
        m_own = E(nc.sbuf_tensor("m_own", [128, NOWN], F32))
        r_own = E(nc.sbuf_tensor("r_own", [128, NOWN], F32))
        ltm = E(nc.sbuf_tensor("ltm", [128, NE * NE], F32))
        comb_tm = ltm
        sm = E(nc.sbuf_tensor("sm", [128, 168], F32))
        xbar = E(nc.sbuf_tensor("xbar", [128, KC], BF16))
        vbar = E(nc.sbuf_tensor("vbar", [128, KC], BF16))
        abar = E(nc.sbuf_tensor("abar", [128, KC], F32))

        pb = [E(nc.psum_tensor(f"pb{i}", [128, 512], F32)) for i in range(8)]

        class Ctr:
            __slots__ = ("sem", "n")
            def __init__(self, name):
                self.sem = E(nc.semaphore(name))
                self.n = 0
            def inc(self, k=1):
                self.n += k
                return self.n

        c_xin = Ctr("xin"); c_const = Ctr("const"); c_prep = Ctr("prep"); c_prepv = Ctr("prepv")
        c_stld = [Ctr("stld0"), Ctr("stld1")]; c_stxf = Ctr("stxf"); c_stch = Ctr("stch")
        c_stT = Ctr("stT")
        c_guld = [Ctr("guld0"), Ctr("guld1"), Ctr("guld2")]; c_gu = Ctr("gu")
        c_dnld = [Ctr("dnld0"), Ctr("dnld1")]; c_dnmm = Ctr("dnmm")
        c_wvld = [Ctr("wvld0"), Ctr("wvld1")]; c_wvmm = Ctr("wvmm")
        c_xnT = Ctr("xnTs"); c_xnv = Ctr("xnv")
        c_tp = Ctr("tp"); c_mbc = Ctr("mbcs")
        c_lgt = Ctr("lgt"); c_ltm = Ctr("ltms"); c_soft = Ctr("soft")
        c_ctp = Ctr("ctp"); c_combT = Ctr("combTs"); c_cbc = Ctr("cbcs")
        c_act = Ctr("acts")
        c_sevac = Ctr("sevac"); c_str = Ctr("str"); c_vch = Ctr("vch")
        c_mrow = Ctr("mrow"); c_sexp = Ctr("sexp"); c_sgd = Ctr("sgd")
        c_sq = Ctr("sq"); c_s12 = Ctr("s12"); c_xnr = Ctr("xnr")
        c_out = [Ctr("outs0"), Ctr("outs1")]; c_fin = Ctr("fin")
        setc = [Ctr(f"setf{i}") for i in range(3)]

        SETS = [(0, 1), (2, 3), (4, 5)]
        set_expected = [0, 0, 0]
        set_rr = [0]

        def acquire_set(n_frees):
            i = set_rr[0] % 3
            set_rr[0] += 1
            wait_val = set_expected[i]
            set_expected[i] += n_frees
            return i, wait_val

        streams = {k: [] for k in ("tensor", "vector", "scalar", "sync", "gpsimd")}
        def em(eng, fn):
            streams[eng].append(fn)

        def _s0(q):
            q.dma_start(out=s_gw[:], in_=gwt[:, :]).then_inc(c_const.sem, 16)
            q.dma_start(out=s_bv[:], in_=bvv[:, :]).then_inc(c_const.sem, 16)
            q.dma_start(out=s_bo[:], in_=bov[:, :]).then_inc(c_const.sem, 16)
            q.dma_start(out=s_eye[:], in_=eyec[:, :]).then_inc(c_const.sem, 16)
            q.dma_start(out=s_onesf[:], in_=onesc[:, :]).then_inc(c_const.sem, 16)
        em("sync", _s0)
        CONST_ALL = c_const.inc(80)

        def _p1(q, w=CONST_ALL):
            q.wait_ge(c_const.sem, w)
            q.copy(s_onec_bf[:], s_onesf[:]).then_inc(c_prep.sem, 1)
        em("scalar", _p1); P_ONEC = c_prep.inc()
        def _p2(q, w=CONST_ALL):
            q.wait_ge(c_const.sem, w)
            q.copy(s_ones_bf[:], s_onesf[:, 0:1].broadcast_to([128, 128])).then_inc(c_prep.sem, 1)
        em("scalar", _p2); P_ONES = c_prep.inc()

        sgn_row = sm[32:33, 160:168]
        gset, gwait = acquire_set(2)
        GB0, GB1 = SETS[gset]
        def _sgm(q, w=CONST_ALL):
            q.wait_ge(c_const.sem, w)
            for kc in range(KC):
                inst = q.matmul(pb[6][0:1, 0:NE], s_onesf[:, 0:1],
                                s_gw[:, kc * NE:(kc + 1) * NE],
                                start=(kc == 0), stop=(kc == KC - 1),
                                skip_group_check=True)
            inst.then_inc(c_lgt.sem, 1)
        em("tensor", _sgm)
        SG_MM = c_lgt.inc()
        def _sge(q, w=SG_MM):
            q.wait_ge(c_lgt.sem, w)
            q.mul(sgn_row, pb[6][0:1, 0:NE], -1.0).then_inc(c_sgd.sem, 1)
        em("scalar", _sge)
        LGT_ALL = SG_MM + KC

        st_ld_val = {}
        def emit_stats_dma(g):
            need = g - 9 if g >= 10 else None
            def _d(q, g=g, need=need):
                if need is not None:
                    q.wait_ge(c_stxf.sem, need)
                if g in (NOWN, NOWN + 1):
                    q.wait_ge(c_xnT.sem, KC)
                q.dma_start(out=stx[g % 2][:], in_=x_stats[g, :, :]).then_inc(c_stld[g % 2].sem, 16)
            em("sync", _d)
            st_ld_val[g] = (g % 2, c_stld[g % 2].inc(16))

        def own_xbuf(g):
            flat = act[g // 4][:].rearrange("p f t -> p (f t)")
            return flat[:, (g % 4) * 2048:(g % 4 + 1) * 2048]

        def stats_xbuf(g):
            return own_xbuf(g) if g < NOWN else xn_tm[g % 2][:, :]

        def emit_stats_group(g0):
            for j, g in enumerate((g0, g0 + 1)):
                slot = g % 2
                ldw = st_ld_val[g]
                xb = stats_xbuf(g)
                def _a(q, slot=slot, ldw=ldw, j=j, w=g0 - NOWN, first=(j == 0), g=g, xb=xb):
                    q.wait_ge(c_stld[ldw[0]].sem, ldw[1])
                    if first and g > NOWN + 1:
                        q.wait_ge(c_stT.sem, w)
                    q.activation(xb, stx[slot][:, :], AF.Copy,
                                 accum_out=stB[:, 0 + j:1 + j]).then_inc(c_stch.sem, 1)
                em("scalar", _a)
                c_stch.inc()
                def _a3(q, slot=slot, j=j, w=c_stch.n, xb=xb):
                    q.wait_ge(c_stch.sem, w)
                    q.activation(xb, stx[slot][:, :], AF.Square,
                                 accum_out=stB[:, 2 + j:3 + j]).then_inc(c_stch.sem, 1)
                em("scalar", _a3)
                c_stch.inc()
            hA = c_stch.n
            def _c(q, w=hA):
                q.wait_ge(c_stch.sem, w)
                q.mul(stB[:, 4:6], stB[:, 0:2], 1.0 / D)
                q.mul(stB[:, 6:8], stB[:, 2:4], 1.0 / D).then_inc(c_stch.sem, 1)
            em("scalar", _c)
            h = c_stch.inc()
            def _d(q, w=h):
                q.wait_ge(c_stch.sem, w)
                q.tensor_tensor(stB[:, 8:10], stB[:, 4:6], stB[:, 4:6], ALU.mult).then_inc(c_stch.sem, 1)
            em("vector", _d); h = c_stch.inc()
            def _e(q, w=h):
                q.wait_ge(c_stch.sem, w)
                q.tensor_tensor(stB[:, 10:12], stB[:, 6:8], stB[:, 8:10], ALU.subtract).then_inc(c_stch.sem, 1)
            em("vector", _e); h = c_stch.inc()
            def _f(q, w=h):
                q.wait_ge(c_stch.sem, w)
                q.tensor_scalar(stB[:, 10:12], stB[:, 10:12], EPS, None, ALU.add).then_inc(c_stch.sem, 1)
            em("vector", _f); h = c_stch.inc()
            def _g(q, w=h):
                q.wait_ge(c_stch.sem, w)
                q.tensor_scalar(stB[:, 12:14], stB[:, 10:12], -0.5, 1.5, ALU.mult, ALU.add).then_inc(c_stch.sem, 1)
            em("vector", _g); h = c_stch.inc()
            for _it in range(NEWTON):
                def _n1(q, w=h):
                    q.wait_ge(c_stch.sem, w)
                    q.tensor_tensor(stB[:, 14:16], stB[:, 12:14], stB[:, 12:14], ALU.mult).then_inc(c_stch.sem, 1)
                em("vector", _n1); h = c_stch.inc()
                def _n2(q, w=h):
                    q.wait_ge(c_stch.sem, w)
                    q.tensor_tensor(stB[:, 16:18], stB[:, 10:12], stB[:, 14:16], ALU.mult).then_inc(c_stch.sem, 1)
                em("vector", _n2); h = c_stch.inc()
                def _n3(q, w=h):
                    q.wait_ge(c_stch.sem, w)
                    q.tensor_scalar(stB[:, 18:20], stB[:, 16:18], -0.5, 1.5, ALU.mult, ALU.add).then_inc(c_stch.sem, 1)
                em("vector", _n3); h = c_stch.inc()
                def _n4(q, w=h):
                    q.wait_ge(c_stch.sem, w)
                    q.tensor_tensor(stB[:, 12:14], stB[:, 12:14], stB[:, 18:20], ALU.mult).then_inc(c_stch.sem, 1)
                em("vector", _n4); h = c_stch.inc()
            def _h(q, w=h):
                q.wait_ge(c_stch.sem, w)
                q.scalar_tensor_tensor(stB[:, 20:22], stB[:, 4:6], -1.0, stB[:, 12:14],
                                       ALU.mult, ALU.mult).then_inc(c_stch.sem, 1)
            em("vector", _h)
            hH = c_stch.inc()
            outs = []
            for j, g in enumerate((g0, g0 + 1)):
                slot = g % 2
                own = g < NOWN
                def _i(q, w=hH, slot=slot, own=own, g=g, j=j, xb=stats_xbuf(g)):
                    q.wait_ge(c_stch.sem, w)
                    if own:
                        q.copy(m_own[:, g:g + 1], stB[:, 4 + j:5 + j])
                        q.copy(r_own[:, g:g + 1], stB[:, 12 + j:13 + j])
                    q.activation(xb, stx[slot][:, :], AF.Identity,
                                 scale=stB[:, 12 + j:13 + j],
                                 bias=stB[:, 20 + j:21 + j]).then_inc(c_stxf.sem, 1)
                em("scalar", _i)
                outs.append(c_stxf.inc())
            return outs

        stats_started = {0: False}
        xn_ready = {}

        def emit_stats_mms(g):
            bank = 6
            first = not stats_started[0]
            stats_started[0] = True
            last = (g == NCH - 1)
            xr = xn_ready[g]
            def _t(q, bank=bank, first=first, last=last, xr=xr, g=g, pones=P_ONEC):
                q.wait_ge(c_stxf.sem, xr)
                if first:
                    q.wait_ge(c_prep.sem, pones)
                    q.wait_ge(c_sgd.sem, 1)
                xbuf = stats_xbuf(g)
                for nt in range(4):
                    inst = q.matmul(pb[bank][32 * nt:32 * nt + 1, 0:512],
                                    s_onec_bf[:], xbuf[:, nt * 512:(nt + 1) * 512],
                                    start=first, stop=last,
                                    skip_group_check=True,
                                    tile_position=(0, 32 * nt))
                inst.then_inc(c_stT.sem, 1)
            em("tensor", _t)
            c_stT.inc()

        for kcq in range(4):
            def _sx(q, kcq=kcq):
                if kcq:
                    q.wait_ge(c_xin.sem, 16 * kcq)
                q.dma_start(out=y[:, 4 * kcq:4 * kcq + 4, :],
                            in_=x_own[:, 4 * kcq:4 * kcq + 4, :]).then_inc(c_xin.sem, 16)
            em("sync", _sx)
            c_xin.inc(16)
        siS, wvS = acquire_set(2)
        siQ, wvQ = acquire_set(2)
        def sqs(kc):
            return stx[(kc // 2) % 2][:, (kc % 2) * TPC:(kc % 2 + 1) * TPC]
        for kc in range(KC):
            def _sq(q, kc=kc):
                q.wait_ge(c_xin.sem, 16 * (kc // 4 + 1))
                if kc >= 4:
                    q.wait_ge(c_s12.sem, kc - 3)
                q.tensor_tensor(sqs(kc), y[:, kc, :], y[:, kc, :],
                                ALU.mult).then_inc(c_sq.sem, 1)
            em("vector", _sq)
            c_sq.inc()
        for kc in range(KC):
            def _s12(q, kc=kc, wS=wvS, wQ=wvQ, cw=CONST_ALL):
                if kc == 0:
                    q.wait_ge(setc[siS].sem, wS)
                    q.wait_ge(setc[siQ].sem, wQ)
                    q.wait_ge(c_const.sem, cw)
                q.wait_ge(c_sq.sem, kc + 1)
                sc = sqs(kc)
                q.matmul(pb[SETS[siS][0]][0:1, 0:512], s_onesf[:, 0:1],
                         y[:, kc, 0:512], start=(kc == 0), stop=(kc == KC - 1),
                         skip_group_check=True)
                q.matmul(pb[SETS[siS][1]][0:1, 0:512], s_onesf[:, 0:1],
                         y[:, kc, 512:1024], start=(kc == 0), stop=(kc == KC - 1),
                         skip_group_check=True)
                q.matmul(pb[SETS[siQ][0]][0:1, 0:512], s_onesf[:, 0:1],
                         sc[:, 0:512], start=(kc == 0), stop=(kc == KC - 1),
                         skip_group_check=True)
                inst = q.matmul(pb[SETS[siQ][1]][0:1, 0:512], s_onesf[:, 0:1],
                                sc[:, 512:1024], start=(kc == 0), stop=(kc == KC - 1),
                                skip_group_check=True)
                inst.then_inc(c_s12.sem, 1)
            em("tensor", _s12)
            c_s12.inc()
            def _xg(q, kc=kc, gw0=gwait):
                if kc == 0:
                    q.wait_ge(setc[gset].sem, gw0)
                q.wait_ge(c_xin.sem, 16 * (kc // 4 + 1))
                q.matmul(pb[GB0][0:NE, 0:512], s_gw[:, kc * NE:(kc + 1) * NE],
                         y[:, kc, 0:512], start=(kc == 0), stop=(kc == KC - 1),
                         skip_group_check=True)
                inst = q.matmul(pb[GB1][0:NE, 0:512], s_gw[:, kc * NE:(kc + 1) * NE],
                                y[:, kc, 512:1024], start=(kc == 0), stop=(kc == KC - 1),
                                skip_group_check=True)
                inst.then_inc(c_lgt.sem, 1)
            em("tensor", _xg)
            c_lgt.inc()
        def _lev(q, w=LGT_ALL):
            q.wait_ge(c_lgt.sem, w)
            q.copy(otile[0][0:NE, 0:512], pb[GB0][0:NE, 0:512]).then_inc(setc[gset].sem, 1)
        em("scalar", _lev)
        def _lev2(q, w=LGT_ALL):
            q.wait_ge(c_lgt.sem, w)
            q.copy(otile[1][0:NE, 0:512], pb[GB1][0:NE, 0:512]).then_inc(setc[gset].sem, 1)
        em("scalar", _lev2)
        LROW_DONE = gwait + 2
        S12_ALL = c_s12.n
        def _sev(q, w=S12_ALL):
            q.wait_ge(c_s12.sem, w)
            q.copy(otile[0][64:65, 0:512], pb[SETS[siS][0]][0:1, 0:512]).then_inc(setc[siS].sem, 1)
            q.copy(otile[1][64:65, 0:512], pb[SETS[siS][1]][0:1, 0:512]).then_inc(setc[siS].sem, 1)
            q.copy(otile[0][96:97, 0:512], pb[SETS[siQ][0]][0:1, 0:512]).then_inc(setc[siQ].sem, 1)
            q.copy(otile[1][96:97, 0:512], pb[SETS[siQ][1]][0:1, 0:512]).then_inc(setc[siQ].sem, 1)
        em("scalar", _sev)
        siT, wvT = acquire_set(2)
        tb1, tb2 = SETS[siT]
        def _str(q, wT=wvT, wS=wvS + 2, wQ=wvQ + 2):
            q.wait_ge(setc[siT].sem, wT)
            q.wait_ge(setc[siS].sem, wS)
            q.wait_ge(setc[siQ].sem, wQ)
            for g in range(NOWN):
                q.matmul(pb[tb1][0:128, g:g + 1],
                         otile[g // 4][64:65, (g % 4) * 128:(g % 4 + 1) * 128],
                         s_eye[64:65, 64:65], start=(g == 0), stop=(g == NOWN - 1),
                         tile_position=(64, 0), skip_group_check=True)
            for g in range(NOWN):
                inst = q.matmul(pb[tb2][0:128, g:g + 1],
                                otile[g // 4][96:97, (g % 4) * 128:(g % 4 + 1) * 128],
                                s_eye[96:97, 96:97], start=(g == 0), stop=(g == NOWN - 1),
                                tile_position=(96, 0), skip_group_check=True)
            inst.then_inc(c_stch.sem, 1)
        em("tensor", _str)
        hT = c_stch.inc()
        def _oev(q, w=hT):
            q.wait_ge(c_stch.sem, w)
            q.copy(oB[:, 0:8], pb[tb1][0:128, 0:8]).then_inc(setc[siT].sem, 1)
            q.copy(oB[:, 8:16], pb[tb2][0:128, 0:8]).then_inc(setc[siT].sem, 1)
        em("scalar", _oev)
        def _oc(q, w=wvT + 2):
            q.wait_ge(setc[siT].sem, w)
            q.mul(oB[:, 16:24], oB[:, 0:8], 1.0 / D)
            q.mul(oB[:, 24:32], oB[:, 8:16], 1.0 / D).then_inc(c_stch.sem, 1)
        em("scalar", _oc)
        h = c_stch.inc()
        def _od1(q, w=h):
            q.wait_ge(c_stch.sem, w)
            q.tensor_tensor(oB[:, 32:40], oB[:, 16:24], oB[:, 16:24], ALU.mult).then_inc(c_stch.sem, 1)
        em("vector", _od1); h = c_stch.inc()
        def _od2(q, w=h):
            q.wait_ge(c_stch.sem, w)
            q.tensor_tensor(oB[:, 40:48], oB[:, 24:32], oB[:, 32:40], ALU.subtract).then_inc(c_stch.sem, 1)
        em("vector", _od2); h = c_stch.inc()
        def _od3(q, w=h):
            q.wait_ge(c_stch.sem, w)
            q.tensor_scalar(oB[:, 40:48], oB[:, 40:48], EPS, None, ALU.add).then_inc(c_stch.sem, 1)
        em("vector", _od3); h = c_stch.inc()
        def _od4(q, w=h):
            q.wait_ge(c_stch.sem, w)
            q.tensor_scalar(oB[:, 48:56], oB[:, 40:48], -0.5, 1.5, ALU.mult, ALU.add).then_inc(c_stch.sem, 1)
        em("vector", _od4); h = c_stch.inc()
        for _it in range(NEWTON):
            def _on1(q, w=h):
                q.wait_ge(c_stch.sem, w)
                q.tensor_tensor(oB[:, 56:64], oB[:, 48:56], oB[:, 48:56], ALU.mult).then_inc(c_stch.sem, 1)
            em("vector", _on1); h = c_stch.inc()
            def _on2(q, w=h):
                q.wait_ge(c_stch.sem, w)
                q.tensor_tensor(oB[:, 64:72], oB[:, 40:48], oB[:, 56:64], ALU.mult).then_inc(c_stch.sem, 1)
            em("vector", _on2); h = c_stch.inc()
            def _on3(q, w=h):
                q.wait_ge(c_stch.sem, w)
                q.tensor_scalar(oB[:, 72:80], oB[:, 64:72], -0.5, 1.5, ALU.mult, ALU.add).then_inc(c_stch.sem, 1)
            em("vector", _on3); h = c_stch.inc()
            def _on4(q, w=h):
                q.wait_ge(c_stch.sem, w)
                q.tensor_tensor(oB[:, 48:56], oB[:, 48:56], oB[:, 72:80], ALU.mult).then_inc(c_stch.sem, 1)
            em("vector", _on4); h = c_stch.inc()
        def _ost(q, w=h):
            q.wait_ge(c_stch.sem, w)
            q.copy(m_own[:, :], oB[:, 16:24])
            q.copy(r_own[:, :], oB[:, 48:56]).then_inc(c_stch.sem, 1)
        em("scalar", _ost)
        H_OWN = c_stch.inc()
        OWN_DONE = [H_OWN + 0]
        for which, src in ((0, m_own), (1, r_own)):
            for cch in range(NOWN):
                si, wv = acquire_set(1)
                bank = SETS[si][cch % 2]
                def _tr(q, si=si, wv=wv, bank=bank, src=src, cch=cch, cw=CONST_ALL,
                        hw=None):
                    q.wait_ge(setc[si].sem, wv)
                    q.wait_ge(c_stch.sem, OWN_DONE[0])
                    q.wait_ge(c_const.sem, cw)
                    q.transpose(pb[bank][0:1, 0:128], src[:, cch:cch + 1],
                                s_eye[:]).then_inc(c_tp.sem, 1)
                em("tensor", _tr)
                tpv = c_tp.inc()
                def _ev(q, tpv=tpv, bank=bank, which=which, cch=cch, si=si):
                    q.wait_ge(c_tp.sem, tpv)
                    row = 32 * which
                    if which == 0:
                        q.copy(otile[cch // 4][32:33, (cch % 4) * 128:(cch % 4 + 1) * 128],
                               pb[bank][0:1, 0:128]).then_inc(c_mrow.sem, 1)
                    q.copy(cbc[0][row:row + 1, cch * 128:(cch + 1) * 128],
                           pb[bank][0:1, 0:128]).then_inc(setc[si].sem, 1)
                em("scalar", _ev)
                if which == 0:
                    c_mrow.inc()
        for which, dst in ((0, mbc), (1, rbc)):
            for nt in range(NT):
                si, wv = acquire_set(1)
                bank = SETS[si][nt]
                def _k1(q, si=si, wv=wv, bank=bank, which=which, nt=nt, pp=P_ONES):
                    q.wait_ge(setc[si].sem, wv)
                    q.wait_ge(c_prep.sem, pp)
                    row = 32 * which
                    q.matmul(pb[bank][:, 0:512], s_ones_bf[row:row + 1, :],
                             cbc[0][row:row + 1, nt * 512:(nt + 1) * 512],
                             start=True, stop=True,
                             tile_position=(row, 0)).then_inc(c_tp.sem, 1)
                em("tensor", _k1)
                tpv = c_tp.inc()
                def _ev(q, tpv=tpv, bank=bank, dst=dst, nt=nt, si=si):
                    q.wait_ge(c_tp.sem, tpv)
                    q.copy(dst[:, nt * 512:(nt + 1) * 512],
                           pb[bank][:, 0:512]).then_inc(setc[si].sem, 1)
                em("scalar", _ev)
        MBC_W = [set_expected[i] for i in range(3)]
        def _mbcdone(q, w=tuple(MBC_W)):
            for i in range(3):
                q.wait_ge(setc[i].sem, w[i])
            q.engine_nop().then_inc(c_mbc.sem, 1)
        em("vector", _mbcdone)
        MBC_DONE = c_mbc.inc()

        def xns(kc):
            return stx[0][:, (kc % 2) * TPC:(kc % 2 + 1) * TPC]
        for kc in range(KC):
            def _v1(q, kc=kc, w=MBC_DONE, xv=c_xin.n, sw=S12_ALL):
                q.wait_ge(c_mbc.sem, w)
                q.wait_ge(c_xin.sem, xv)
                if kc >= 2:
                    q.wait_ge(c_xnT.sem, kc - 1)
                else:
                    q.wait_ge(c_s12.sem, sw)
                q.tensor_tensor(xns(kc), y[:, kc, :], mbc[:, :], ALU.subtract).then_inc(c_xnv.sem, 1)
            em("vector", _v1); v1 = c_xnv.inc()
            def _v2(q, kc=kc, w=v1):
                q.wait_ge(c_xnv.sem, w)
                q.tensor_tensor(xns(kc), xns(kc), rbc[:, :], ALU.mult).then_inc(c_xnv.sem, 1)
            em("vector", _v2); v2 = c_xnv.inc()
            def _sc(q, kc=kc, w=v2):
                q.wait_ge(c_xnv.sem, w)
                q.copy(xnT[:, kc, :], xns(kc)).then_inc(c_xnT.sem, 1)
            em("scalar", _sc)
            c_xnT.inc()

        LOGIT = {}
        soft_phases = []

        def _mk_softmax():
            LT = LOGIT["ltm_w"]
            P = [[] for _ in range(12)]
            def col(base, tt, n=1):
                return sm[:, base + tt:base + tt + n]
            for tt in range(NE):
                L = ltm[:, tt * NE:(tt + 1) * NE]
                a = sm[:, 32 + 8 * tt:40 + 8 * tt]
                b = sm[:, 96 + 8 * tt:104 + 8 * tt]
                def f0(q, L=L, mx=col(0, tt), w=LT):
                    for i in range(3):
                        q.wait_ge(setc[i].sem, w[i])
                    q.reduce_max(mx, L, axis=AX.X).then_inc(c_soft.sem, 1)
                P[0].append(("vector", f0))
            B0 = c_soft.inc(NE)
            for tt in range(NE):
                L = ltm[:, tt * NE:(tt + 1) * NE]
                a = sm[:, 32 + 8 * tt:40 + 8 * tt]
                def f1(q, L=L, a=a, mx=col(0, tt), w=B0):
                    q.wait_ge(c_soft.sem, w)
                    q.tensor_scalar(a, L, mx, None, ALU.subtract).then_inc(c_soft.sem, 1)
                P[1].append(("vector", f1))
            B1 = c_soft.inc(NE)
            for tt in range(NE):
                a = sm[:, 32 + 8 * tt:40 + 8 * tt]
                b = sm[:, 96 + 8 * tt:104 + 8 * tt]
                def f2(q, a=a, b=b, w=B1):
                    q.wait_ge(c_soft.sem, w)
                    q.activation(b, a, AF.Exp).then_inc(c_sexp.sem, 1)
                P[2].append(("scalar", f2))
            BX = c_sexp.inc(NE)
            for tt in range(NE):
                b = sm[:, 96 + 8 * tt:104 + 8 * tt]
                def f3(q, b=b, sumc=col(8, tt), w=BX):
                    q.wait_ge(c_sexp.sem, w)
                    q.reduce_sum(sumc, b, axis=AX.X).then_inc(c_soft.sem, 1)
                P[3].append(("vector", f3))
            B3 = c_soft.inc(NE)
            def f4(q, w=B3):
                q.wait_ge(c_soft.sem, w)
                q.reciprocal(sm[:, 16:24], sm[:, 8:16]).then_inc(c_soft.sem, 1)
            P[4].append(("vector", f4))
            B4 = c_soft.inc()
            prevB = B4
            specs = [
                lambda tt, a, b, w: lambda q, a=a, b=b, rc=col(16, tt), w=w: (
                    q.wait_ge(c_soft.sem, w),
                    q.tensor_scalar(a, b, rc, None, ALU.mult).then_inc(c_soft.sem, 1)),
                lambda tt, a, b, w: lambda q, a=a, mx=col(0, tt), w=w: (
                    q.wait_ge(c_soft.sem, w),
                    q.reduce_max(mx, a, axis=AX.X).then_inc(c_soft.sem, 1)),
                lambda tt, a, b, w: lambda q, a=a, b=b, mx=col(0, tt), w=w: (
                    q.wait_ge(c_soft.sem, w),
                    q.tensor_scalar(b, a, mx, None, ALU.is_equal).then_inc(c_soft.sem, 1)),
                lambda tt, a, b, w: lambda q, a=a, b=b, w=w: (
                    q.wait_ge(c_soft.sem, w),
                    q.scalar_tensor_tensor(b, b, -1e30, a, ALU.mult, ALU.add).then_inc(c_soft.sem, 1)),
                lambda tt, a, b, w: lambda q, b=b, m2=col(24, tt), w=w: (
                    q.wait_ge(c_soft.sem, w),
                    q.reduce_max(m2, b, axis=AX.X).then_inc(c_soft.sem, 1)),
                lambda tt, a, b, w: lambda q, a=a, b=b, m2=col(24, tt), w=w: (
                    q.wait_ge(c_soft.sem, w),
                    q.tensor_scalar(b, a, m2, None, ALU.is_ge).then_inc(c_soft.sem, 1)),
                lambda tt, a, b, w: lambda q, a=a, b=b, tt=tt, w=w: (
                    q.wait_ge(c_soft.sem, w),
                    q.tensor_tensor(comb_tm[:, tt * NE:(tt + 1) * NE], a, b,
                                    ALU.mult).then_inc(c_soft.sem, 1)),
            ]
            for pi, mk in enumerate(specs):
                for tt in range(NE):
                    a = sm[:, 32 + 8 * tt:40 + 8 * tt]
                    b = sm[:, 96 + 8 * tt:104 + 8 * tt]
                    P[5 + pi].append(("vector", mk(tt, a, b, prevB)))
                prevB = c_soft.inc(NE)
            LOGIT["soft_all"] = c_soft.n
            for grp in ([P[0], P[1]], [P[2]], [P[3], P[4], P[5]],
                        [P[6], P[7], P[8]], [P[9], P[10], P[11]]):
                batch = []
                for ph in grp:
                    batch.extend(ph)
                soft_phases.append(batch)

        def emit_logit_finish():
            si, wv = acquire_set(8)
            bA, bB = SETS[si]
            def _ltr(q, si=si, wv=wv, ld=LROW_DONE):
                q.wait_ge(setc[gset].sem, ld)
                q.wait_ge(c_mrow.sem, NOWN)
                q.wait_ge(c_sgd.sem, 1)
                q.wait_ge(setc[si].sem, wv)
                for tt in range(NE):
                    bank = bA if tt % 2 == 0 else bB
                    coff = 8 * (tt // 2)
                    src = otile[tt // 4]
                    col = (tt % 4) * 128
                    q.matmul(pb[bank][0:128, coff:coff + 8],
                             src[0:NE, col:col + 128],
                             s_eye[0:NE, 0:NE], start=(tt < 2), stop=False,
                             skip_group_check=True)
                    inst = q.matmul(pb[bank][0:128, coff:coff + 8],
                                    src[32:33, col:col + 128],
                                    sgn_row,
                                    start=False, stop=(tt >= 6),
                                    tile_position=(32, 0),
                                    skip_group_check=True)
                inst.then_inc(c_tp.sem, 1)
            em("tensor", _ltr)
            tpv = c_tp.inc()
            for tt in range(NE):
                bank = bA if tt % 2 == 0 else bB
                coff = 8 * (tt // 2)
                def _lte(q, tpv=tpv, bank=bank, coff=coff, tt=tt, si=si):
                    q.wait_ge(c_tp.sem, tpv)
                    q.tensor_scalar(ltm[:, tt * NE:(tt + 1) * NE],
                                    pb[bank][0:128, coff:coff + 8],
                                    r_own[:, tt:tt + 1], None,
                                    ALU.mult).then_inc(setc[si].sem, 1)
                em("vector", _lte)
            LOGIT["ltr_tp"] = c_tp.n
            LOGIT["ltm_w"] = tuple(set_expected[i] for i in range(3))
            _mk_softmax()

        emit_logit_finish()
        while soft_phases:
            for eng, fn in soft_phases.pop(0):
                em(eng, fn)

        XBOX = {}
        for kc in range(KC):
            def _xr(q, kc=kc):
                q.wait_ge(c_xnT.sem, kc + 1)
                q.reduce_sum(xaccs[:, kc:kc + 1], xnT[:, kc, :],
                             axis=AX.X).then_inc(c_xnr.sem, 1)
            em("vector", _xr)
            c_xnr.inc()
        def _xsc(q, w=c_xnr.n):
            q.wait_ge(c_xnr.sem, w)
            q.tensor_scalar(xaccs[:, :], xaccs[:, :], 1.0 / SEQ, None,
                            ALU.mult).then_inc(c_xnr.sem, 1)
        em("vector", _xsc)
        XBOX["xacc"] = c_xnr.inc()

        def combT_row(e):
            t = mbc if e < 4 else rbc
            return t, 32 * (e % 4)
        def emit_comb_transposes(e):
            for tt in range(NE):
                si, wv = acquire_set(1)
                bank = SETS[si][(e + tt) % 2]
                def _ctr(q, si=si, wv=wv, bank=bank, e=e, tt=tt,
                         w=LOGIT["soft_all"]):
                    q.wait_ge(c_soft.sem, w)
                    q.wait_ge(c_xnT.sem, KC)
                    q.wait_ge(setc[si].sem, wv)
                    q.transpose(pb[bank][0:1, 0:128],
                                comb_tm[:, tt * NE + e:tt * NE + e + 1],
                                s_eye[:]).then_inc(c_ctp.sem, 1)
                em("tensor", _ctr)
                cv = c_ctp.inc()
                def _cte(q, cv=cv, bank=bank, e=e, tt=tt, si=si):
                    q.wait_ge(c_ctp.sem, cv)
                    t, row = combT_row(e)
                    q.copy(t[row:row + 1, tt * 128:(tt + 1) * 128],
                           pb[bank][0:1, 0:128]).then_inc(setc[si].sem, 1)
                em("scalar", _cte)

        COMBT_W_BOX = {}

        def emit_cbc(e):
            actw = 22 * (e - 1) if e >= 2 else None
            ready = []
            for nt in range(NT):
                si, wv = acquire_set(1)
                bank = SETS[si][nt]
                t, row = combT_row(e)
                def _k(q, si=si, wv=wv, bank=bank, nt=nt, t=t, row=row,
                       e=e, pp=P_ONES, first=(nt == 0)):
                    if first:
                        for i in range(3):
                            q.wait_ge(setc[i].sem, COMBT_W_BOX[e][i])
                    q.wait_ge(c_prep.sem, pp)
                    q.wait_ge(setc[si].sem, wv)
                    q.matmul(pb[bank][:, 0:512], s_ones_bf[row:row + 1, :],
                             t[row:row + 1, nt * 512:(nt + 1) * 512],
                             start=True, stop=True,
                             tile_position=(row, 0)).then_inc(c_tp.sem, 1)
                em("tensor", _k)
                tpv = c_tp.inc()
                def _ke(q, tpv=tpv, bank=bank, e=e, nt=nt, si=si, actw=actw):
                    q.wait_ge(c_tp.sem, tpv)
                    if actw is not None:
                        q.wait_ge(c_act.sem, actw)
                    q.copy(cbc[e % 2][:, nt * 512:(nt + 1) * 512],
                           pb[bank][:, 0:512]).then_inc(setc[si].sem, 1)
                em("scalar", _ke)
                ready.append((si, wv + 1))
            return ready

        cbc_ready = {}

        gu_ld_val = {}
        gu_w_idx = [0]
        def emit_gu_dma(e, f, j):
            w = gu_w_idx[0]; gu_w_idx[0] += 1
            if e < NE:
                src = guw[e][j * FC + f]
            elif e == NE:
                src = sguw[j * 2 * FC + f]
            else:
                src = sguw[j * 2 * FC + FC + f]
            need = w - 2 if w >= 3 else None
            def _d(q, src=src, slot=w % 3, need=need):
                if need is not None:
                    q.wait_ge(c_gu.sem, need)
                q.dma_start(out=gub[slot][:], in_=src[:, :]).then_inc(c_guld[w % 3].sem, 16)
            em("gpsimd", _d)
            gu_ld_val[w] = (w % 3, c_guld[w % 3].inc(16))

        dn_ld_val = {}
        dn_s_idx = [0]
        def emit_dn_dma(e, hcx):
            s = dn_s_idx[0]; dn_s_idx[0] += 1
            need = s - 1 if s >= 2 else None
            def _d(q, e=e, hcx=hcx, slot=s % 2, need=need):
                if need is not None:
                    q.wait_ge(c_dnmm.sem, need)
                q.dma_start(out=dnb[slot][:], in_=dnw[e][hcx, :, :]).then_inc(c_dnld[s % 2].sem, 16)
            em("gpsimd", _d)
            dn_ld_val[s] = (s % 2, c_dnld[s % 2].inc(16))
            return s

        wv_ld_val = {}
        wv_s_idx = [0]
        def emit_wv_dma(which, half_idx):
            s = wv_s_idx[0]; wv_s_idx[0] += 1
            src = wvt if which == 0 else wot
            need = s // 2 if s >= 2 else None
            def _d(q, src=src, half_idx=half_idx, slot=s % 2, need=need):
                if need is not None:
                    q.wait_ge(c_wvmm.sem, need)
                q.dma_start(out=wvb[slot][:], in_=src[half_idx, :, :]).then_inc(c_wvld[s % 2].sem, 16)
            em("gpsimd", _d)
            wv_ld_val[s] = (s % 2, c_wvld[s % 2].inc(16))
            return s

        def emit_stats_evac():
            stTall = c_stT.n
            for nt in range(4):
                def _ev(q, nt=nt, w=stTall, lt=LOGIT["ltr_tp"], hw=H_OWN):
                    q.wait_ge(c_stT.sem, w)
                    q.wait_ge(c_tp.sem, lt)
                    q.wait_ge(c_stch.sem, hw)
                    q.activation(otile[0][32 * nt:32 * nt + 1, 0:512],
                                 pb[6][32 * nt:32 * nt + 1, 0:512],
                                 AF.Identity, scale=1.0 / SEQ).then_inc(c_sevac.sem, 1)
                em("scalar", _ev)
                c_sevac.inc()
            return c_sevac.n

        def emit_xbar():
            sev = c_sevac.n
            prev_ev = [None]
            for kc in range(KC):
                nt = kc // 4
                col = (kc % 4) * 128
                base = 32 * nt
                pe = prev_ev[0]
                def _tr(q, base=base, col=col, w=sev, pe=pe):
                    q.wait_ge(c_sevac.sem, w)
                    if pe is not None:
                        q.wait_ge(c_str.sem, pe)
                    q.transpose(pb[6][:, 1:2],
                                otile[0][base:base + 1, col:col + 128],
                                s_eye[base:base + 1, base:base + 1],
                                tile_position=(base, 0)).then_inc(c_str.sem, 1)
                em("tensor", _tr)
                tv = c_str.inc()
                def _ev(q, tv=tv, kc=kc, xw=XBOX["xacc"]):
                    q.wait_ge(c_str.sem, tv)
                    q.wait_ge(c_xnr.sem, xw)
                    q.activation(xbar[:, kc:kc + 1], pb[6][:, 1:2], AF.Identity,
                                 bias=xaccs[:, kc:kc + 1]).then_inc(c_str.sem, 1)
                em("scalar", _ev)
                prev_ev[0] = c_str.inc()
            return c_str.n

        def emit_vblock(which, kc, xbar_ready, vbar_ready):
            s0 = emit_wv_dma(which, 2 * kc)
            s1 = emit_wv_dma(which, 2 * kc + 1)
            def _mm(q, kc=kc, s0=s0, s1=s1, which=which, xr=xbar_ready, vr=vbar_ready):
                q.wait_ge(c_wvld[wv_ld_val[s0][0]].sem, wv_ld_val[s0][1])
                q.wait_ge(c_wvld[wv_ld_val[s1][0]].sem, wv_ld_val[s1][1])
                if xr is not None:
                    q.wait_ge(c_str.sem, xr)
                if vr is not None:
                    q.wait_ge(c_vch.sem, vr)
                mov = xbar if which == 0 else vbar
                for oc in range(16):
                    slot = (s0 + (oc // 8)) % 2
                    inst = q.matmul(pb[7][:, oc:oc + 1],
                                    wvb[slot][:, (oc % 8) * 128:(oc % 8 + 1) * 128],
                                    mov[:, kc:kc + 1],
                                    start=(kc == 0 and oc == 0),
                                    stop=(kc == KC - 1 and oc == 15),
                                    skip_group_check=True)
                inst.then_inc(c_wvmm.sem, 1)
            em("tensor", _mm)
            c_wvmm.inc()

        def emit_vevac(which):
            wall = c_wvmm.n
            for kc in range(KC):
                def _ev(q, which=which, kc=kc, w=wall):
                    q.wait_ge(c_wvmm.sem, w)
                    bias = s_bv if which == 0 else s_bo
                    dst = vbar if which == 0 else abar
                    q.activation(dst[:, kc:kc + 1],
                                 pb[7][:, kc:kc + 1], AF.Identity,
                                 bias=bias[:, kc:kc + 1]).then_inc(c_vch.sem, 1)
                em("scalar", _ev)
                c_vch.inc()
            return c_vch.n

        stat_dma_ptr = [NOWN]
        stat_chain_ptr = [NOWN]
        stat_mm_ptr = [NOWN]
        wvp = {"xbar": None, "vmm": 0, "vevac": None, "amm": 0, "aevac": None}

        def stats_slot(pair_parity):
            if pair_parity == 0 and stat_chain_ptr[0] < NCH:
                g = stat_chain_ptr[0]; stat_chain_ptr[0] += 2
                while stat_dma_ptr[0] < min(NCH, g + 4):
                    emit_stats_dma(stat_dma_ptr[0]); stat_dma_ptr[0] += 1
                r0, r1 = emit_stats_group(g)
                xn_ready[g] = r0
                xn_ready[g + 1] = r1
            lim = stat_chain_ptr[0] - (2 if stat_chain_ptr[0] < NCH else 0)
            if stat_mm_ptr[0] < lim:
                emit_stats_mms(stat_mm_ptr[0])
                emit_stats_mms(stat_mm_ptr[0] + 1)
                stat_mm_ptr[0] += 2

        def stats_done():
            return stat_mm_ptr[0] >= NCH

        def vbar_slot():
            if wvp["xbar"] is None:
                emit_stats_evac()
                wvp["xbar"] = emit_xbar()
            elif wvp["vmm"] < KC:
                emit_vblock(0, wvp["vmm"], wvp["xbar"] if wvp["vmm"] == 0 else None, None)
                wvp["vmm"] += 1
            elif wvp["vevac"] is None:
                wvp["vevac"] = emit_vevac(0)
            elif wvp["amm"] < KC:
                emit_vblock(1, wvp["amm"], None, wvp["vevac"])
                wvp["amm"] += 1
            elif wvp["aevac"] is None:
                wvp["aevac"] = emit_vevac(1)

        half_idx = [0]
        iter9_hc_free = {}
        deferred_m2 = []
        prev_m1 = [None]
        NW = NIT * FC * 2
        def emit_gu_chunk_dma(w):
            if w >= NW:
                return
            pidx, j = divmod(w, 2)
            ee, ff = divmod(pidx, FC)
            emit_gu_dma(ee, ff, j)

        for e in range(NIT):
            abuf = act[e % 2]
            for f in range(FC):
                pslot = 2 * (e * FC + f)
                if not (f == 0 and e > 0):
                    emit_gu_chunk_dma(pslot)
                    emit_gu_chunk_dma(pslot + 1)
                if 3 <= f <= 4:
                    emit_dn_dma(e, f - 3)
                pair_info = []
                for j in range(2):
                    w = half_idx[0]; half_idx[0] += 1
                    si, wv = acquire_set(2)
                    pair_info.append((si, wv))
                    def _g(q, e=e, w=w, si=si, wv=wv, chase=(e == 0)):
                        q.wait_ge(c_guld[gu_ld_val[w][0]].sem, gu_ld_val[w][1])
                        q.wait_ge(setc[si].sem, wv)
                        if e == 0 and not chase:
                            q.wait_ge(c_xnT.sem, KC)
                        slot = w % 3
                        b0, b1 = SETS[si]
                        for kc in range(KC):
                            if chase:
                                q.wait_ge(c_xnT.sem, kc + 1)
                            stat = gub[slot][:, kc * 128:(kc + 1) * 128]
                            q.matmul(pb[b0][:, 0:512], stat, xnT[:, kc, 0:512],
                                     start=(kc == 0), stop=(kc == KC - 1),
                                     skip_group_check=True)
                            inst = q.matmul(pb[b1][:, 0:512], stat, xnT[:, kc, 512:1024],
                                            start=(kc == 0), stop=(kc == KC - 1),
                                            skip_group_check=True)
                        inst.then_inc(c_gu.sem, 1)
                    em("tensor", _g)
                    c_gu.inc()
                gdone = c_gu.n
                (siA, wvA), (siB, wvB) = pair_info
                pm = prev_m1[0]
                for nt in range(NT):
                    bA = SETS[siA][nt]
                    def _si(q, bA=bA, nt=nt, w=gdone - 1, siA=siA, pm=pm):
                        q.wait_ge(c_gu.sem, w)
                        if pm is not None:
                            q.wait_ge(setc[pm[0]].sem, pm[1] + nt + 1)
                        q.activation(silu_s[:, nt * 512:(nt + 1) * 512],
                                     pb[bA][:, 0:512], AF.Silu).then_inc(setc[siA].sem, 1)
                    em("scalar", _si)
                for nt in range(NT):
                    bB = SETS[siB][nt]
                    wA = wvA + nt + 1
                    def _m1(q, bB=bB, nt=nt, f=f, abuf=abuf, siA=siA, siB=siB,
                            wA=wA, w=gdone,
                            dprev=(16 * (e - 1) if (e >= 2 and f == 0 and nt == 0) else None)):
                        q.wait_ge(setc[siA].sem, wA)
                        q.wait_ge(c_gu.sem, w)
                        if dprev is not None:
                            q.wait_ge(c_dnmm.sem, dprev)
                        q.tensor_tensor(abuf[:, f, nt * 512:(nt + 1) * 512],
                                        silu_s[:, nt * 512:(nt + 1) * 512],
                                        pb[bB][:, 0:512], ALU.mult).then_inc(setc[siB].sem, 1)
                    em("vector", _m1)
                    wB = wvB + nt + 1
                    if e < NE:
                        def _m2(q, e=e, f=f, nt=nt, abuf=abuf, siB=siB, wB=wB):
                            q.wait_ge(setc[siB].sem, wB)
                            for csi, cvv in cbc_ready[e]:
                                q.wait_ge(setc[csi].sem, cvv)
                            q.tensor_tensor(abuf[:, f, nt * 512:(nt + 1) * 512],
                                            abuf[:, f, nt * 512:(nt + 1) * 512],
                                            cbc[e % 2][:, nt * 512:(nt + 1) * 512],
                                            ALU.mult).then_inc(c_act.sem, 1)
                        if e == 0:
                            deferred_m2.append(_m2)
                        else:
                            em("vector", _m2)
                    else:
                        def _m2(q, siB=siB, wB=wB):
                            q.wait_ge(setc[siB].sem, wB)
                            q.engine_nop().then_inc(c_act.sem, 1)
                        em("vector", _m2)
                    c_act.inc()
                prev_m1[0] = (siB, wvB)
                if e == 0:
                    if f in (9, 10):
                        for ee in range(4 * (f - 9), 4 * (f - 8)):
                            emit_comb_transposes(ee)
                            COMBT_W_BOX[ee] = tuple(set_expected[i] for i in range(3))
                if e == 0:
                    pass
                elif not stats_done():
                    stats_slot(0 if (f % 2 == 0 and f <= 8) else 1)
                else:
                    vbar_slot()
            if e == 0:
                cbc_ready[0] = emit_cbc(0)
                cbc_ready[1] = emit_cbc(1)
                for fn2 in deferred_m2:
                    em("vector", fn2)
                deferred_m2.clear()
            elif e + 1 < NE and (e + 1) not in cbc_ready:
                cbc_ready[e + 1] = emit_cbc(e + 1)
            emit_gu_chunk_dma(2 * (e + 1) * FC)
            emit_gu_chunk_dma(2 * (e + 1) * FC + 1)
            ACT_ALL = c_act.n
            for hc in range(HCN):
                if hc < 14:
                    emit_dn_dma(e, hc + 2)
                s = e * HCN + hc
                if True:
                    si, wv = acquire_set(2)
                    def _dn(q, s=s, si=si, wv=wv, aw=ACT_ALL, abuf=abuf, hc0=(hc == 0),
                            e=e, xw=(c_xin.n if e == 0 else None)):
                        if xw is not None:
                            q.wait_ge(c_xin.sem, xw)
                        q.wait_ge(c_dnld[dn_ld_val[s][0]].sem, dn_ld_val[s][1])
                        if not hc0:
                            q.wait_ge(c_act.sem, aw)
                        q.wait_ge(setc[si].sem, wv)
                        slot = s % 2
                        b0, b1 = SETS[si]
                        for fc in range(FC):
                            if hc0:
                                q.wait_ge(c_act.sem, 22 * e + 2 * (fc + 1))
                            stat = dnb[slot][:, fc * 128:(fc + 1) * 128]
                            q.matmul(pb[b0][:, 0:512], stat, abuf[:, fc, 0:512],
                                     start=(fc == 0), stop=(fc == FC - 1),
                                     skip_group_check=True)
                            inst = q.matmul(pb[b1][:, 0:512], stat, abuf[:, fc, 512:1024],
                                            start=(fc == 0), stop=(fc == FC - 1),
                                            skip_group_check=True)
                        inst.then_inc(c_dnmm.sem, 1)
                    em("tensor", _dn)
                    dmv = c_dnmm.inc()
                    for nt in range(NT):
                        b = SETS[si][nt]
                        def _ye(q, b=b, hc=hc, nt=nt, w=dmv, si=si,
                                lg=(LGT_ALL if e == 0 else None)):
                            q.wait_ge(c_dnmm.sem, w)
                            if lg is not None:
                                q.wait_ge(c_lgt.sem, lg)
                            q.tensor_tensor(y[:, hc, nt * 512:(nt + 1) * 512],
                                            y[:, hc, nt * 512:(nt + 1) * 512],
                                            pb[b][:, 0:512], ALU.add).then_inc(setc[si].sem, 1)
                        em("vector", _ye)
                    if e == NIT - 1:
                        iter9_hc_free[hc] = (si, set_expected[si])

        while not stats_done():
            stats_slot(0)
        while wvp["aevac"] is None:
            vbar_slot()
        ABAR = wvp["aevac"]
        XBAR_CHAIN = wvp["xbar"]

        oidx = [0]
        for hc in range(HCN):
            si, freev = iter9_hc_free[hc]
            for half in range(2):
                i = oidx[0]; oidx[0] += 1
                slot = i % 2
                def _fa(q, hc=hc, half=half, si=si, freev=freev, slot=slot, i=i,
                        aw=ABAR, xc=XBAR_CHAIN):
                    q.wait_ge(setc[si].sem, freev)
                    q.wait_ge(c_vch.sem, aw)
                    if i < 2:
                        q.wait_ge(c_str.sem, xc)
                    if i >= 2:
                        q.wait_ge(c_out[i % 2].sem, 16 * ((i - 2) // 2 + 1))
                    q.activation(otile[slot][:, 0:512], y[:, hc, half * 512:(half + 1) * 512],
                                 AF.Identity,
                                 bias=abar[:, hc:hc + 1]).then_inc(c_fin.sem, 1)
                em("scalar", _fa)
                fv = c_fin.inc()
                def _od(q, hc=hc, half=half, slot=slot, w=fv):
                    q.wait_ge(c_fin.sem, w)
                    q.dma_start(out=y_out[:, hc, half * 512:(half + 1) * 512],
                                in_=otile[slot][:, 0:512]).then_inc(c_out[slot].sem, 16)
                em("sync", _od)
                c_out[slot].inc(16)
        def _fini(q, w0=None, w1=None):
            q.wait_ge(c_out[0].sem, c_out[0].n)
            q.wait_ge(c_out[1].sem, c_out[1].n)
        em("sync", _fini)

        with nc.Block() as block:
            @block.sync
            def _(q):
                for fn in streams["sync"]:
                    fn(q)
            @block.gpsimd
            def _(q):
                for fn in streams["gpsimd"]:
                    fn(q)
            @block.tensor
            def _(q):
                for fn in streams["tensor"]:
                    fn(q)
            @block.scalar
            def _(q):
                for fn in streams["scalar"]:
                    fn(q)
            @block.vector
            def _(q):
                for fn in streams["vector"]:
                    fn(q)
    return nc


def _layouts(inputs):
    f32 = np.float32
    x = np.asarray(inputs["L0_attn_x"], f32)
    shared = {}
    for e in range(NE):
        A = np.asarray(inputs["L0_ffn_gate_up_weights"][e], f32)
        T = np.ascontiguousarray(A.T)
        C = T.reshape(KC, 128, 2 * FC, 128).transpose(2, 1, 0, 3)
        shared[f"guw{e}"] = np.ascontiguousarray(C).reshape(2 * FC, 128, KC * 128)
    Asg = np.asarray(inputs["L0_ffn_shared_gate_up"], f32)
    Tsg = np.ascontiguousarray(Asg.T)
    Csg = Tsg.reshape(KC, 128, 4 * FC, 128).transpose(2, 1, 0, 3)
    shared["sguw"] = np.ascontiguousarray(Csg).reshape(4 * FC, 128, KC * 128)
    for e in range(NE):
        Dw = np.asarray(inputs["L0_ffn_down_weights"][e], f32)
        Td = np.ascontiguousarray(Dw.T)
        G = Td.reshape(FC, 128, 16, 128).transpose(2, 1, 0, 3)
        shared[f"dnw{e}"] = np.ascontiguousarray(G).reshape(16, 128, FC * 128)
    Sd = np.asarray(inputs["L0_ffn_shared_down"], f32)
    Tsd = np.ascontiguousarray(Sd.T)
    for k, half in ((8, Tsd[:1408]), (9, Tsd[1408:])):
        G = np.ascontiguousarray(half).reshape(FC, 128, 16, 128).transpose(2, 1, 0, 3)
        shared[f"dnw{k}"] = np.ascontiguousarray(G).reshape(16, 128, FC * 128)
    Wv = np.asarray(inputs["L0_attn_Wv"], f32)
    Wo = np.asarray(inputs["L0_attn_Wo"], f32)
    shared["wvt"] = np.ascontiguousarray(
        np.ascontiguousarray(Wv.T).reshape(KC, 128, 2, 1024).transpose(0, 2, 1, 3)
    ).reshape(KC * 2, 128, 1024)
    shared["wot"] = np.ascontiguousarray(
        np.ascontiguousarray(Wo.T).reshape(KC, 128, 2, 1024).transpose(0, 2, 1, 3)
    ).reshape(KC * 2, 128, 1024)
    gw = np.asarray(inputs["L0_ffn_gate_weight"], f32)
    shared["gwt"] = np.ascontiguousarray(gw.T.reshape(KC, 128, NE).transpose(1, 0, 2)
                                         ).reshape(128, KC * NE)
    shared["bvv"] = np.ascontiguousarray(np.asarray(inputs["L0_attn_bv"], f32)
                                         .reshape(KC, 128).T)
    shared["bov"] = np.ascontiguousarray(np.asarray(inputs["L0_attn_bo"], f32)
                                         .reshape(KC, 128).T)
    shared["eyec"] = np.eye(128, dtype=f32)
    shared["onesc"] = np.ones((128, 1), f32)

    in_maps = []
    for c in range(NCORES):
        bb = c // 4
        sl0 = slice((c % 4) * TPC, (c % 4 + 1) * TPC)
        X = x[bb, sl0]
        xo = np.ascontiguousarray(X.T.reshape(KC, 128, TPC).transpose(1, 0, 2))
        own = [(c % 4) * 8 + i for i in range(8)]
        rest = [g for g in range(NCH) if g not in own]
        order = own + rest
        xs = np.ascontiguousarray(x[bb].reshape(NCH, 128, D)[order])
        m = dict(shared)
        m["x_own"] = xo
        m["x_stats"] = xs
        in_maps.append(m)
    return in_maps


def kernel(**inputs):
    in_maps = _layouts(inputs)
    nc = build_program()
    res = run_bass_kernel_spmd(nc, in_maps, core_ids=list(range(NCORES)))
    out = np.empty((NB, SEQ, D), np.float32)
    for c in range(NCORES):
        yT = np.asarray(res.results[c]["y_out"])
        Y = yT.transpose(1, 0, 2).reshape(D, TPC)
        bb = c // 4
        sl0 = slice((c % 4) * TPC, (c % 4 + 1) * TPC)
        out[bb, sl0] = Y.T
    return out


# version 18 (final)
# speedup vs baseline: 1.0339x; 1.0339x over previous
import contextlib
import numpy as np
import concourse.bass as bass
import concourse.mybir as mybir
from concourse.bass_utils import run_bass_kernel_spmd

F32 = mybir.dt.float32
BF16 = mybir.dt.bfloat16
AF = mybir.ActivationFunctionType
ALU = mybir.AluOpType
AX = mybir.AxisListType

D = 2048
KC = 16
TPC = 1024
NT = 2
TN = 512
NE = 8
NIT = 10
FC = 11
HCN = 16
NCH = 32
NOWN = 8
EPS = 1e-5
SEQ = 4096
NB = 2
NEWTON = 4
NCORES = 8


def build_program():
    nc = bass.Bass("TRN2", target_bir_lowering=False)

    x_own = nc.dram_tensor("x_own", [128, KC, TPC], F32, kind="ExternalInput")
    x_stats = nc.dram_tensor("x_stats", [NCH, 128, D], F32, kind="ExternalInput")
    guw = [nc.dram_tensor(f"guw{e}", [2 * FC, 128, KC * 128], F32, kind="ExternalInput")
           for e in range(NE)]
    sguw = nc.dram_tensor("sguw", [4 * FC, 128, KC * 128], F32, kind="ExternalInput")
    dnw = [nc.dram_tensor(f"dnw{e}", [16, 128, FC * 128], F32, kind="ExternalInput")
           for e in range(NIT)]
    wvt = nc.dram_tensor("wvt", [KC * 2, 128, 1024], F32, kind="ExternalInput")
    wot = nc.dram_tensor("wot", [KC * 2, 128, 1024], F32, kind="ExternalInput")
    gwt = nc.dram_tensor("gwt", [128, KC * NE], F32, kind="ExternalInput")
    bvv = nc.dram_tensor("bvv", [128, KC], F32, kind="ExternalInput")
    bov = nc.dram_tensor("bov", [128, KC], F32, kind="ExternalInput")
    eyec = nc.dram_tensor("eyec", [128, 128], F32, kind="ExternalInput")
    onesc = nc.dram_tensor("onesc", [128, 1], F32, kind="ExternalInput")
    y_out = nc.dram_tensor("y_out", [128, KC, TPC], F32, kind="ExternalOutput")

    with contextlib.ExitStack() as ctx:
        E = ctx.enter_context

        y = E(nc.sbuf_tensor("y", [128, KC, TPC], F32))
        xnT = E(nc.sbuf_tensor("xnT", [128, KC, TPC], BF16))
        act = [E(nc.sbuf_tensor(f"act{i}", [128, FC, TPC], BF16)) for i in range(2)]
        gub = [E(nc.sbuf_tensor(f"gub{i}", [128, KC * 128], BF16)) for i in range(3)]
        dnb = [E(nc.sbuf_tensor(f"dnb{i}", [128, FC * 128], BF16)) for i in range(2)]
        stx = [E(nc.sbuf_tensor(f"stx{i}", [128, D], F32)) for i in range(2)]
        xn_tm = [E(nc.sbuf_tensor(f"xn_tm{i}", [128, D], BF16)) for i in range(2)]
        wvb = [E(nc.sbuf_tensor(f"wvb{i}", [128, 1024], BF16)) for i in range(2)]
        mbc = E(nc.sbuf_tensor("mbc", [128, TPC], BF16))
        rbc = E(nc.sbuf_tensor("rbc", [128, TPC], BF16))
        silu_s = E(nc.sbuf_tensor("silu_s", [128, TPC], F32))
        cbc = [E(nc.sbuf_tensor(f"cbc{i}", [128, TPC], BF16)) for i in range(2)]
        otile = [E(nc.sbuf_tensor(f"otile{i}", [128, TN], F32)) for i in range(2)]
        s_gw = E(nc.sbuf_tensor("s_gw", [128, KC * NE], F32))
        s_bv = E(nc.sbuf_tensor("s_bv", [128, KC], F32))
        s_bo = E(nc.sbuf_tensor("s_bo", [128, KC], F32))
        s_eye = E(nc.sbuf_tensor("s_eye", [128, 128], F32))
        s_onesf = E(nc.sbuf_tensor("s_onesf", [128, 1], F32))
        s_ones_bf = E(nc.sbuf_tensor("s_ones_bf", [128, 128], BF16))
        s_onec_bf = E(nc.sbuf_tensor("s_onec_bf", [128, 1], BF16))
        stB = E(nc.sbuf_tensor("stB", [128, 32], F32))
        oB = E(nc.sbuf_tensor("oB", [128, 96], F32))
        o_rbf = E(nc.sbuf_tensor("o_rbf", [128, 8], BF16))
        o_mrbf = E(nc.sbuf_tensor("o_mrbf", [128, 8], BF16))
        st_corr = E(nc.sbuf_tensor("st_corr", [128, 1], F32))
        m_own = E(nc.sbuf_tensor("m_own", [128, NOWN], F32))
        r_own = E(nc.sbuf_tensor("r_own", [128, NOWN], F32))
        ltm = E(nc.sbuf_tensor("ltm", [128, NE * NE], F32))
        comb_tm = ltm
        sm = E(nc.sbuf_tensor("sm", [128, 168], F32))
        xbar = E(nc.sbuf_tensor("xbar", [128, KC], BF16))
        vbar = E(nc.sbuf_tensor("vbar", [128, KC], BF16))
        abar = E(nc.sbuf_tensor("abar", [128, KC], F32))

        pb = [E(nc.psum_tensor(f"pb{i}", [128, 512], F32)) for i in range(8)]

        class Ctr:
            __slots__ = ("sem", "n")
            def __init__(self, name):
                self.sem = E(nc.semaphore(name))
                self.n = 0
            def inc(self, k=1):
                self.n += k
                return self.n

        c_xin = Ctr("xin"); c_const = Ctr("const"); c_prep = Ctr("prep"); c_prepv = Ctr("prepv")
        c_stld = [Ctr("stld0"), Ctr("stld1")]; c_stxf = Ctr("stxf"); c_stch = Ctr("stch")
        c_old = [Ctr(f"old{i}") for i in range(NOWN)]
        c_stT = Ctr("stT")
        c_guld = [Ctr("guld0"), Ctr("guld1"), Ctr("guld2")]; c_gu = Ctr("gu")
        c_dnld = [Ctr("dnld0"), Ctr("dnld1")]; c_dnmm = Ctr("dnmm")
        c_wvld = [Ctr("wvld0"), Ctr("wvld1")]; c_wvmm = Ctr("wvmm")
        c_xnT = Ctr("xnTs"); c_xnv = Ctr("xnv")
        c_tp = Ctr("tp"); c_mbc = Ctr("mbcs")
        c_lgt = Ctr("lgt"); c_ltm = Ctr("ltms"); c_soft = Ctr("soft")
        c_ctp = Ctr("ctp"); c_combT = Ctr("combTs"); c_cbc = Ctr("cbcs")
        c_act = Ctr("acts")
        c_sevac = Ctr("sevac"); c_str = Ctr("str"); c_vch = Ctr("vch")
        c_mrow = Ctr("mrow"); c_sexp = Ctr("sexp")
        c_out = [Ctr("outs0"), Ctr("outs1")]; c_fin = Ctr("fin")
        setc = [Ctr(f"setf{i}") for i in range(3)]

        SETS = [(0, 1), (2, 3), (4, 5)]
        set_expected = [0, 0, 0]
        set_rr = [0]

        def acquire_set(n_frees):
            i = set_rr[0] % 3
            set_rr[0] += 1
            wait_val = set_expected[i]
            set_expected[i] += n_frees
            return i, wait_val

        streams = {k: [] for k in ("tensor", "vector", "scalar", "sync", "gpsimd")}
        def em(eng, fn):
            streams[eng].append(fn)

        def _s0(q):
            q.dma_start(out=s_gw[:], in_=gwt[:, :]).then_inc(c_const.sem, 16)
            q.dma_start(out=s_bv[:], in_=bvv[:, :]).then_inc(c_const.sem, 16)
            q.dma_start(out=s_bo[:], in_=bov[:, :]).then_inc(c_const.sem, 16)
            q.dma_start(out=s_eye[:], in_=eyec[:, :]).then_inc(c_const.sem, 16)
            q.dma_start(out=s_onesf[:], in_=onesc[:, :]).then_inc(c_const.sem, 16)
        em("sync", _s0)
        CONST_ALL = c_const.inc(80)

        def _p1(q, w=CONST_ALL):
            q.wait_ge(c_const.sem, w)
            q.copy(s_onec_bf[:], s_onesf[:]).then_inc(c_prep.sem, 1)
        em("scalar", _p1); P_ONEC = c_prep.inc()
        def _p2(q, w=CONST_ALL):
            q.wait_ge(c_const.sem, w)
            q.copy(s_ones_bf[:], s_onesf[:, 0:1].broadcast_to([128, 128])).then_inc(c_prep.sem, 1)
        em("scalar", _p2); P_ONES = c_prep.inc()

        sgn_row = sm[32:33, 160:168]
        gset, gwait = acquire_set(2)
        GB0, GB1 = SETS[gset]
        si_sg, wv_sg = acquire_set(1)
        bank_sg = SETS[si_sg][0]
        def _sgm(q, w=CONST_ALL, wv=wv_sg):
            q.wait_ge(c_const.sem, w)
            q.wait_ge(setc[si_sg].sem, wv)
            for kc in range(KC):
                inst = q.matmul(pb[bank_sg][0:1, 0:NE], s_onesf[:, 0:1],
                                s_gw[:, kc * NE:(kc + 1) * NE],
                                start=(kc == 0), stop=(kc == KC - 1),
                                skip_group_check=True)
            inst.then_inc(c_lgt.sem, 1)
        em("tensor", _sgm)
        SG_MM = c_lgt.inc()
        def _sge(q, w=SG_MM):
            q.wait_ge(c_lgt.sem, w)
            q.mul(sgn_row, pb[bank_sg][0:1, 0:NE], -1.0).then_inc(setc[si_sg].sem, 1)
        em("scalar", _sge)
        SGN_W = wv_sg + 1
        for kc in range(KC):
            def _xg(q, kc=kc, w=CONST_ALL, gw0=gwait):
                if kc == 0:
                    q.wait_ge(c_const.sem, w)
                    q.wait_ge(setc[gset].sem, gw0)
                q.wait_ge(c_xin.sem, 16 * (kc // 4 + 1))
                q.matmul(pb[GB0][0:NE, 0:512], s_gw[:, kc * NE:(kc + 1) * NE],
                         y[:, kc, 0:512], start=(kc == 0), stop=(kc == KC - 1),
                         skip_group_check=True)
                inst = q.matmul(pb[GB1][0:NE, 0:512], s_gw[:, kc * NE:(kc + 1) * NE],
                                y[:, kc, 512:1024], start=(kc == 0), stop=(kc == KC - 1),
                                skip_group_check=True)
                inst.then_inc(c_lgt.sem, 1)
            em("tensor", _xg)
            c_lgt.inc()
        LGT_ALL = c_lgt.n

        st_ld_val = {}
        def emit_stats_dma(g):
            need = g - 9 if g >= 10 else None
            def _d(q, g=g, need=need):
                if need is not None:
                    q.wait_ge(c_stxf.sem, need)
                if g in (NOWN, NOWN + 1):
                    q.wait_ge(c_xnT.sem, KC)
                q.dma_start(out=stx[g % 2][:], in_=x_stats[g, :, :]).then_inc(c_stld[g % 2].sem, 16)
            em("sync", _d)
            st_ld_val[g] = (g % 2, c_stld[g % 2].inc(16))

        def own_xbuf(g):
            flat = act[g // 4][:].rearrange("p f t -> p (f t)")
            return flat[:, (g % 4) * 2048:(g % 4 + 1) * 2048]

        def stats_xbuf(g):
            return own_xbuf(g) if g < NOWN else xn_tm[g % 2][:, :]

        def emit_stats_group(g0):
            for j, g in enumerate((g0, g0 + 1)):
                slot = g % 2
                ldw = st_ld_val[g]
                xb = stats_xbuf(g)
                def _a(q, slot=slot, ldw=ldw, j=j, w=g0, first=(j == 0), g=g, xb=xb):
                    q.wait_ge(c_stld[ldw[0]].sem, ldw[1])
                    if first and g > NOWN + 1:
                        q.wait_ge(c_stT.sem, w)
                    q.activation(xb, stx[slot][:, :], AF.Copy,
                                 accum_out=stB[:, 0 + j:1 + j]).then_inc(c_stch.sem, 1)
                em("scalar", _a)
                c_stch.inc()
                def _a3(q, slot=slot, j=j, w=c_stch.n, xb=xb):
                    q.wait_ge(c_stch.sem, w)
                    q.activation(xb, stx[slot][:, :], AF.Square,
                                 accum_out=stB[:, 2 + j:3 + j]).then_inc(c_stch.sem, 1)
                em("scalar", _a3)
                c_stch.inc()
            hA = c_stch.n
            def _c(q, w=hA):
                q.wait_ge(c_stch.sem, w)
                q.mul(stB[:, 4:6], stB[:, 0:2], 1.0 / D)
                q.mul(stB[:, 6:8], stB[:, 2:4], 1.0 / D).then_inc(c_stch.sem, 1)
            em("scalar", _c)
            h = c_stch.inc()
            def _d(q, w=h):
                q.wait_ge(c_stch.sem, w)
                q.tensor_tensor(stB[:, 8:10], stB[:, 4:6], stB[:, 4:6], ALU.mult).then_inc(c_stch.sem, 1)
            em("vector", _d); h = c_stch.inc()
            def _e(q, w=h):
                q.wait_ge(c_stch.sem, w)
                q.tensor_tensor(stB[:, 10:12], stB[:, 6:8], stB[:, 8:10], ALU.subtract).then_inc(c_stch.sem, 1)
            em("vector", _e); h = c_stch.inc()
            def _f(q, w=h):
                q.wait_ge(c_stch.sem, w)
                q.tensor_scalar(stB[:, 10:12], stB[:, 10:12], EPS, None, ALU.add).then_inc(c_stch.sem, 1)
            em("vector", _f); h = c_stch.inc()
            def _g(q, w=h):
                q.wait_ge(c_stch.sem, w)
                q.tensor_scalar(stB[:, 12:14], stB[:, 10:12], -0.5, 1.5, ALU.mult, ALU.add).then_inc(c_stch.sem, 1)
            em("vector", _g); h = c_stch.inc()
            for _it in range(NEWTON):
                def _n1(q, w=h):
                    q.wait_ge(c_stch.sem, w)
                    q.tensor_tensor(stB[:, 14:16], stB[:, 12:14], stB[:, 12:14], ALU.mult).then_inc(c_stch.sem, 1)
                em("vector", _n1); h = c_stch.inc()
                def _n2(q, w=h):
                    q.wait_ge(c_stch.sem, w)
                    q.tensor_tensor(stB[:, 16:18], stB[:, 10:12], stB[:, 14:16], ALU.mult).then_inc(c_stch.sem, 1)
                em("vector", _n2); h = c_stch.inc()
                def _n3(q, w=h):
                    q.wait_ge(c_stch.sem, w)
                    q.tensor_scalar(stB[:, 18:20], stB[:, 16:18], -0.5, 1.5, ALU.mult, ALU.add).then_inc(c_stch.sem, 1)
                em("vector", _n3); h = c_stch.inc()
                def _n4(q, w=h):
                    q.wait_ge(c_stch.sem, w)
                    q.tensor_tensor(stB[:, 12:14], stB[:, 12:14], stB[:, 18:20], ALU.mult).then_inc(c_stch.sem, 1)
                em("vector", _n4); h = c_stch.inc()
            def _h(q, w=h):
                q.wait_ge(c_stch.sem, w)
                q.scalar_tensor_tensor(stB[:, 20:22], stB[:, 4:6], -1.0, stB[:, 12:14],
                                       ALU.mult, ALU.mult).then_inc(c_stch.sem, 1)
            em("vector", _h)
            hH = c_stch.inc()
            outs = []
            for j, g in enumerate((g0, g0 + 1)):
                slot = g % 2
                own = g < NOWN
                def _i(q, w=hH, slot=slot, own=own, g=g, j=j, xb=stats_xbuf(g)):
                    q.wait_ge(c_stch.sem, w)
                    if own:
                        q.copy(m_own[:, g:g + 1], stB[:, 4 + j:5 + j])
                        q.copy(r_own[:, g:g + 1], stB[:, 12 + j:13 + j])
                    q.activation(xb, stx[slot][:, :], AF.Identity,
                                 scale=stB[:, 12 + j:13 + j],
                                 bias=stB[:, 20 + j:21 + j]).then_inc(c_stxf.sem, 1)
                em("scalar", _i)
                outs.append(c_stxf.inc())
            return outs

        stats_started = {0: False}
        xn_ready = {}

        def emit_stats_mms(g):
            bank = 6
            first = not stats_started[0]
            stats_started[0] = True
            last = (g == NCH - 1)
            xr = xn_ready[g]
            def _t(q, bank=bank, first=first, last=last, xr=xr, g=g, pones=P_ONEC):
                q.wait_ge(c_stxf.sem, xr)
                if g == 0:
                    q.wait_ge(c_prep.sem, pones)
                xbuf = stats_xbuf(g)
                for nt in range(4):
                    inst = q.matmul(pb[bank][32 * nt:32 * nt + 1, 0:512],
                                    s_onec_bf[:], xbuf[:, nt * 512:(nt + 1) * 512],
                                    start=first, stop=last,
                                    skip_group_check=True,
                                    tile_position=(0, 32 * nt))
                inst.then_inc(c_stT.sem, 1)
            em("tensor", _t)
            c_stT.inc()

        for g in range(NOWN):
            def _d(q, g=g):
                q.dma_start(out=own_xbuf(g), in_=x_stats[g, :, :]).then_inc(c_old[g].sem, 16)
            em("gpsimd", _d)
            c_old[g].inc(16)
        for kcq in range(4):
            def _sx(q, kcq=kcq):
                if kcq:
                    q.wait_ge(c_xin.sem, 16 * kcq)
                q.dma_start(out=y[:, 4 * kcq:4 * kcq + 4, :],
                            in_=x_own[:, 4 * kcq:4 * kcq + 4, :]).then_inc(c_xin.sem, 16)
            em("sync", _sx)
            c_xin.inc(16)
        OA0 = c_stch.n
        for g in range(NOWN):
            def _oa1(q, g=g, base=OA0):
                q.wait_ge(c_old[g].sem, 16)
                if g:
                    q.wait_ge(c_stch.sem, base + 2 * g)
                q.activation(xn_tm[0][:, :], own_xbuf(g), AF.Copy,
                             accum_out=oB[:, 0 + g:1 + g]).then_inc(c_stch.sem, 1)
            em("scalar", _oa1)
            c_stch.inc()
            def _oa2(q, g=g, w=c_stch.n):
                q.wait_ge(c_stch.sem, w)
                q.activation(xn_tm[0][:, :], own_xbuf(g), AF.Square,
                             accum_out=oB[:, 8 + g:9 + g]).then_inc(c_stch.sem, 1)
            em("scalar", _oa2)
            c_stch.inc()
        hA = c_stch.n
        def _oc(q, w=hA):
            q.wait_ge(c_stch.sem, w)
            q.mul(oB[:, 16:24], oB[:, 0:8], 1.0 / D)
            q.mul(oB[:, 24:32], oB[:, 8:16], 1.0 / D).then_inc(c_stch.sem, 1)
        em("scalar", _oc)
        h = c_stch.inc()
        def _od1(q, w=h):
            q.wait_ge(c_stch.sem, w)
            q.tensor_tensor(oB[:, 32:40], oB[:, 16:24], oB[:, 16:24], ALU.mult).then_inc(c_stch.sem, 1)
        em("vector", _od1); h = c_stch.inc()
        def _od2(q, w=h):
            q.wait_ge(c_stch.sem, w)
            q.tensor_tensor(oB[:, 40:48], oB[:, 24:32], oB[:, 32:40], ALU.subtract).then_inc(c_stch.sem, 1)
        em("vector", _od2); h = c_stch.inc()
        def _od3(q, w=h):
            q.wait_ge(c_stch.sem, w)
            q.tensor_scalar(oB[:, 40:48], oB[:, 40:48], EPS, None, ALU.add).then_inc(c_stch.sem, 1)
        em("vector", _od3); h = c_stch.inc()
        def _od4(q, w=h):
            q.wait_ge(c_stch.sem, w)
            q.tensor_scalar(oB[:, 48:56], oB[:, 40:48], -0.5, 1.5, ALU.mult, ALU.add).then_inc(c_stch.sem, 1)
        em("vector", _od4); h = c_stch.inc()
        for _it in range(NEWTON):
            def _on1(q, w=h):
                q.wait_ge(c_stch.sem, w)
                q.tensor_tensor(oB[:, 56:64], oB[:, 48:56], oB[:, 48:56], ALU.mult).then_inc(c_stch.sem, 1)
            em("vector", _on1); h = c_stch.inc()
            def _on2(q, w=h):
                q.wait_ge(c_stch.sem, w)
                q.tensor_tensor(oB[:, 64:72], oB[:, 40:48], oB[:, 56:64], ALU.mult).then_inc(c_stch.sem, 1)
            em("vector", _on2); h = c_stch.inc()
            def _on3(q, w=h):
                q.wait_ge(c_stch.sem, w)
                q.tensor_scalar(oB[:, 72:80], oB[:, 64:72], -0.5, 1.5, ALU.mult, ALU.add).then_inc(c_stch.sem, 1)
            em("vector", _on3); h = c_stch.inc()
            def _on4(q, w=h):
                q.wait_ge(c_stch.sem, w)
                q.tensor_tensor(oB[:, 48:56], oB[:, 48:56], oB[:, 72:80], ALU.mult).then_inc(c_stch.sem, 1)
            em("vector", _on4); h = c_stch.inc()
        def _omr(q, w=h):
            q.wait_ge(c_stch.sem, w)
            q.tensor_tensor(oB[:, 80:88], oB[:, 16:24], oB[:, 48:56], ALU.mult).then_inc(c_stch.sem, 1)
        em("vector", _omr)
        hMR = c_stch.inc()
        def _ost(q, w=hMR):
            q.wait_ge(c_stch.sem, w)
            q.copy(m_own[:, :], oB[:, 16:24])
            q.copy(r_own[:, :], oB[:, 48:56])
            q.copy(o_rbf[:, :], oB[:, 48:56])
            q.copy(o_mrbf[:, :], oB[:, 80:88]).then_inc(c_stch.sem, 1)
        em("scalar", _ost)
        H_OWN = c_stch.inc()
        def _lev(q, w=LGT_ALL):
            q.wait_ge(c_lgt.sem, w)
            q.copy(otile[0][0:NE, 0:512], pb[GB0][0:NE, 0:512]).then_inc(setc[gset].sem, 1)
        em("scalar", _lev)
        def _lev2(q, w=LGT_ALL):
            q.wait_ge(c_lgt.sem, w)
            q.copy(otile[1][0:NE, 0:512], pb[GB1][0:NE, 0:512]).then_inc(setc[gset].sem, 1)
        em("scalar", _lev2)
        LROW_DONE = gwait + 2
        for g in range(NOWN):
            def _t(q, g=g, w=H_OWN, pones=P_ONES):
                q.wait_ge(c_stch.sem, w)
                if g == 0:
                    q.wait_ge(c_prep.sem, pones)
                xb = own_xbuf(g)
                for nt in range(4):
                    q.matmul(pb[6][32 * nt:32 * nt + 1, 0:512],
                             o_rbf[:, g:g + 1], xb[:, nt * 512:(nt + 1) * 512],
                             start=(g == 0), stop=False,
                             skip_group_check=True,
                             tile_position=(0, 32 * nt))
                inst = q.matmul(pb[7][:, 0:1], s_ones_bf[:, :],
                                o_mrbf[:, g:g + 1],
                                start=(g == 0), stop=(g == NOWN - 1),
                                skip_group_check=True)
                inst.then_inc(c_stT.sem, 1)
            em("tensor", _t)
            c_stT.inc()
        stats_started[0] = True

        OWN_DONE = [H_OWN + 0]
        for which, src in ((0, m_own), (1, r_own)):
            for cch in range(NOWN):
                si, wv = acquire_set(1)
                bank = SETS[si][cch % 2]
                def _tr(q, si=si, wv=wv, bank=bank, src=src, cch=cch, cw=CONST_ALL,
                        hw=None):
                    q.wait_ge(setc[si].sem, wv)
                    q.wait_ge(c_stch.sem, OWN_DONE[0])
                    q.wait_ge(c_const.sem, cw)
                    q.transpose(pb[bank][0:1, 0:128], src[:, cch:cch + 1],
                                s_eye[:]).then_inc(c_tp.sem, 1)
                em("tensor", _tr)
                tpv = c_tp.inc()
                def _ev(q, tpv=tpv, bank=bank, which=which, cch=cch, si=si):
                    q.wait_ge(c_tp.sem, tpv)
                    row = 32 * which
                    if which == 0:
                        q.copy(otile[cch // 4][32:33, (cch % 4) * 128:(cch % 4 + 1) * 128],
                               pb[bank][0:1, 0:128]).then_inc(c_mrow.sem, 1)
                    q.copy(cbc[0][row:row + 1, cch * 128:(cch + 1) * 128],
                           pb[bank][0:1, 0:128]).then_inc(setc[si].sem, 1)
                em("scalar", _ev)
                if which == 0:
                    c_mrow.inc()
        for which, dst in ((0, mbc), (1, rbc)):
            for nt in range(NT):
                si, wv = acquire_set(1)
                bank = SETS[si][nt]
                def _k1(q, si=si, wv=wv, bank=bank, which=which, nt=nt, pp=P_ONES):
                    q.wait_ge(setc[si].sem, wv)
                    q.wait_ge(c_prep.sem, pp)
                    row = 32 * which
                    q.matmul(pb[bank][:, 0:512], s_ones_bf[row:row + 1, :],
                             cbc[0][row:row + 1, nt * 512:(nt + 1) * 512],
                             start=True, stop=True,
                             tile_position=(row, 0)).then_inc(c_tp.sem, 1)
                em("tensor", _k1)
                tpv = c_tp.inc()
                def _ev(q, tpv=tpv, bank=bank, dst=dst, nt=nt, si=si):
                    q.wait_ge(c_tp.sem, tpv)
                    q.copy(dst[:, nt * 512:(nt + 1) * 512],
                           pb[bank][:, 0:512]).then_inc(setc[si].sem, 1)
                em("scalar", _ev)
        MBC_W = [set_expected[i] for i in range(3)]
        def _mbcdone(q, w=tuple(MBC_W)):
            for i in range(3):
                q.wait_ge(setc[i].sem, w[i])
            q.engine_nop().then_inc(c_mbc.sem, 1)
        em("vector", _mbcdone)
        MBC_DONE = c_mbc.inc()

        def xns(kc):
            return stx[0][:, (kc % 2) * TPC:(kc % 2 + 1) * TPC]
        for kc in range(KC):
            def _v1(q, kc=kc, w=MBC_DONE, xv=c_xin.n):
                q.wait_ge(c_mbc.sem, w)
                q.wait_ge(c_xin.sem, xv)
                if kc >= 2:
                    q.wait_ge(c_xnT.sem, kc - 1)
                q.tensor_tensor(xns(kc), y[:, kc, :], mbc[:, :], ALU.subtract).then_inc(c_xnv.sem, 1)
            em("vector", _v1); v1 = c_xnv.inc()
            def _v2(q, kc=kc, w=v1):
                q.wait_ge(c_xnv.sem, w)
                q.tensor_tensor(xns(kc), xns(kc), rbc[:, :], ALU.mult).then_inc(c_xnv.sem, 1)
            em("vector", _v2); v2 = c_xnv.inc()
            def _sc(q, kc=kc, w=v2):
                q.wait_ge(c_xnv.sem, w)
                q.copy(xnT[:, kc, :], xns(kc)).then_inc(c_xnT.sem, 1)
            em("scalar", _sc)
            c_xnT.inc()

        LOGIT = {}
        soft_phases = []

        def _mk_softmax():
            LT = LOGIT["ltm_w"]
            P = [[] for _ in range(12)]
            def col(base, tt, n=1):
                return sm[:, base + tt:base + tt + n]
            for tt in range(NE):
                L = ltm[:, tt * NE:(tt + 1) * NE]
                a = sm[:, 32 + 8 * tt:40 + 8 * tt]
                b = sm[:, 96 + 8 * tt:104 + 8 * tt]
                def f0(q, L=L, mx=col(0, tt), w=LT):
                    for i in range(3):
                        q.wait_ge(setc[i].sem, w[i])
                    q.reduce_max(mx, L, axis=AX.X).then_inc(c_soft.sem, 1)
                P[0].append(("vector", f0))
            B0 = c_soft.inc(NE)
            for tt in range(NE):
                L = ltm[:, tt * NE:(tt + 1) * NE]
                a = sm[:, 32 + 8 * tt:40 + 8 * tt]
                def f1(q, L=L, a=a, mx=col(0, tt), w=B0):
                    q.wait_ge(c_soft.sem, w)
                    q.tensor_scalar(a, L, mx, None, ALU.subtract).then_inc(c_soft.sem, 1)
                P[1].append(("vector", f1))
            B1 = c_soft.inc(NE)
            for tt in range(NE):
                a = sm[:, 32 + 8 * tt:40 + 8 * tt]
                b = sm[:, 96 + 8 * tt:104 + 8 * tt]
                def f2(q, a=a, b=b, w=B1):
                    q.wait_ge(c_soft.sem, w)
                    q.activation(b, a, AF.Exp).then_inc(c_sexp.sem, 1)
                P[2].append(("scalar", f2))
            BX = c_sexp.inc(NE)
            for tt in range(NE):
                b = sm[:, 96 + 8 * tt:104 + 8 * tt]
                def f3(q, b=b, sumc=col(8, tt), w=BX):
                    q.wait_ge(c_sexp.sem, w)
                    q.reduce_sum(sumc, b, axis=AX.X).then_inc(c_soft.sem, 1)
                P[3].append(("vector", f3))
            B3 = c_soft.inc(NE)
            def f4(q, w=B3):
                q.wait_ge(c_soft.sem, w)
                q.reciprocal(sm[:, 16:24], sm[:, 8:16]).then_inc(c_soft.sem, 1)
            P[4].append(("vector", f4))
            B4 = c_soft.inc()
            prevB = B4
            specs = [
                lambda tt, a, b, w: lambda q, a=a, b=b, rc=col(16, tt), w=w: (
                    q.wait_ge(c_soft.sem, w),
                    q.tensor_scalar(a, b, rc, None, ALU.mult).then_inc(c_soft.sem, 1)),
                lambda tt, a, b, w: lambda q, a=a, mx=col(0, tt), w=w: (
                    q.wait_ge(c_soft.sem, w),
                    q.reduce_max(mx, a, axis=AX.X).then_inc(c_soft.sem, 1)),
                lambda tt, a, b, w: lambda q, a=a, b=b, mx=col(0, tt), w=w: (
                    q.wait_ge(c_soft.sem, w),
                    q.tensor_scalar(b, a, mx, None, ALU.is_equal).then_inc(c_soft.sem, 1)),
                lambda tt, a, b, w: lambda q, a=a, b=b, w=w: (
                    q.wait_ge(c_soft.sem, w),
                    q.scalar_tensor_tensor(b, b, -1e30, a, ALU.mult, ALU.add).then_inc(c_soft.sem, 1)),
                lambda tt, a, b, w: lambda q, b=b, m2=col(24, tt), w=w: (
                    q.wait_ge(c_soft.sem, w),
                    q.reduce_max(m2, b, axis=AX.X).then_inc(c_soft.sem, 1)),
                lambda tt, a, b, w: lambda q, a=a, b=b, m2=col(24, tt), w=w: (
                    q.wait_ge(c_soft.sem, w),
                    q.tensor_scalar(b, a, m2, None, ALU.is_ge).then_inc(c_soft.sem, 1)),
                lambda tt, a, b, w: lambda q, a=a, b=b, tt=tt, w=w: (
                    q.wait_ge(c_soft.sem, w),
                    q.tensor_tensor(comb_tm[:, tt * NE:(tt + 1) * NE], a, b,
                                    ALU.mult).then_inc(c_soft.sem, 1)),
            ]
            for pi, mk in enumerate(specs):
                for tt in range(NE):
                    a = sm[:, 32 + 8 * tt:40 + 8 * tt]
                    b = sm[:, 96 + 8 * tt:104 + 8 * tt]
                    P[5 + pi].append(("vector", mk(tt, a, b, prevB)))
                prevB = c_soft.inc(NE)
            LOGIT["soft_all"] = c_soft.n
            for grp in ([P[0], P[1]], [P[2]], [P[3], P[4], P[5]],
                        [P[6], P[7], P[8]], [P[9], P[10], P[11]]):
                batch = []
                for ph in grp:
                    batch.extend(ph)
                soft_phases.append(batch)

        def emit_logit_finish():
            si, wv = acquire_set(8)
            bA, bB = SETS[si]
            def _ltr(q, si=si, wv=wv, ld=LROW_DONE, sw=SGN_W):
                q.wait_ge(setc[gset].sem, ld)
                q.wait_ge(c_mrow.sem, NOWN)
                q.wait_ge(setc[si_sg].sem, sw)
                q.wait_ge(setc[si].sem, wv)
                for tt in range(NE):
                    bank = bA if tt % 2 == 0 else bB
                    coff = 8 * (tt // 2)
                    src = otile[tt // 4]
                    col = (tt % 4) * 128
                    q.matmul(pb[bank][0:128, coff:coff + 8],
                             src[0:NE, col:col + 128],
                             s_eye[0:NE, 0:NE], start=(tt < 2), stop=False,
                             skip_group_check=True)
                    inst = q.matmul(pb[bank][0:128, coff:coff + 8],
                                    src[32:33, col:col + 128],
                                    sgn_row,
                                    start=False, stop=(tt >= 6),
                                    tile_position=(32, 0),
                                    skip_group_check=True)
                inst.then_inc(c_tp.sem, 1)
            em("tensor", _ltr)
            tpv = c_tp.inc()
            for tt in range(NE):
                bank = bA if tt % 2 == 0 else bB
                coff = 8 * (tt // 2)
                def _lte(q, tpv=tpv, bank=bank, coff=coff, tt=tt, si=si):
                    q.wait_ge(c_tp.sem, tpv)
                    q.tensor_scalar(ltm[:, tt * NE:(tt + 1) * NE],
                                    pb[bank][0:128, coff:coff + 8],
                                    r_own[:, tt:tt + 1], None,
                                    ALU.mult).then_inc(setc[si].sem, 1)
                em("vector", _lte)
            LOGIT["ltr_tp"] = c_tp.n
            LOGIT["ltm_w"] = tuple(set_expected[i] for i in range(3))
            _mk_softmax()

        emit_logit_finish()
        while soft_phases:
            for eng, fn in soft_phases.pop(0):
                em(eng, fn)

        def combT_row(e):
            t = mbc if e < 4 else rbc
            return t, 32 * (e % 4)
        def emit_comb_transposes(e):
            for tt in range(NE):
                si, wv = acquire_set(1)
                bank = SETS[si][(e + tt) % 2]
                def _ctr(q, si=si, wv=wv, bank=bank, e=e, tt=tt,
                         w=LOGIT["soft_all"]):
                    q.wait_ge(c_soft.sem, w)
                    q.wait_ge(c_xnT.sem, KC)
                    q.wait_ge(setc[si].sem, wv)
                    q.transpose(pb[bank][0:1, 0:128],
                                comb_tm[:, tt * NE + e:tt * NE + e + 1],
                                s_eye[:]).then_inc(c_ctp.sem, 1)
                em("tensor", _ctr)
                cv = c_ctp.inc()
                def _cte(q, cv=cv, bank=bank, e=e, tt=tt, si=si):
                    q.wait_ge(c_ctp.sem, cv)
                    t, row = combT_row(e)
                    q.copy(t[row:row + 1, tt * 128:(tt + 1) * 128],
                           pb[bank][0:1, 0:128]).then_inc(setc[si].sem, 1)
                em("scalar", _cte)

        COMBT_W_BOX = {}

        def emit_cbc(e):
            actw = 22 * (e - 1) if e >= 2 else None
            ready = []
            for nt in range(NT):
                si, wv = acquire_set(1)
                bank = SETS[si][nt]
                t, row = combT_row(e)
                def _k(q, si=si, wv=wv, bank=bank, nt=nt, t=t, row=row,
                       e=e, pp=P_ONES, first=(nt == 0)):
                    if first:
                        for i in range(3):
                            q.wait_ge(setc[i].sem, COMBT_W_BOX[e][i])
                    q.wait_ge(c_prep.sem, pp)
                    q.wait_ge(setc[si].sem, wv)
                    q.matmul(pb[bank][:, 0:512], s_ones_bf[row:row + 1, :],
                             t[row:row + 1, nt * 512:(nt + 1) * 512],
                             start=True, stop=True,
                             tile_position=(row, 0)).then_inc(c_tp.sem, 1)
                em("tensor", _k)
                tpv = c_tp.inc()
                def _ke(q, tpv=tpv, bank=bank, e=e, nt=nt, si=si, actw=actw):
                    q.wait_ge(c_tp.sem, tpv)
                    if actw is not None:
                        q.wait_ge(c_act.sem, actw)
                    q.copy(cbc[e % 2][:, nt * 512:(nt + 1) * 512],
                           pb[bank][:, 0:512]).then_inc(setc[si].sem, 1)
                em("scalar", _ke)
                ready.append((si, wv + 1))
            return ready

        cbc_ready = {}

        gu_ld_val = {}
        gu_w_idx = [0]
        def emit_gu_dma(e, f, j):
            w = gu_w_idx[0]; gu_w_idx[0] += 1
            if e < NE:
                src = guw[e][j * FC + f]
            elif e == NE:
                src = sguw[j * 2 * FC + f]
            else:
                src = sguw[j * 2 * FC + FC + f]
            need = w - 2 if w >= 3 else None
            def _d(q, src=src, slot=w % 3, need=need):
                if need is not None:
                    q.wait_ge(c_gu.sem, need)
                q.dma_start(out=gub[slot][:], in_=src[:, :]).then_inc(c_guld[w % 3].sem, 16)
            em("gpsimd", _d)
            gu_ld_val[w] = (w % 3, c_guld[w % 3].inc(16))

        dn_ld_val = {}
        dn_s_idx = [0]
        def emit_dn_dma(e, hcx):
            s = dn_s_idx[0]; dn_s_idx[0] += 1
            need = s - 1 if s >= 2 else None
            def _d(q, e=e, hcx=hcx, slot=s % 2, need=need):
                if need is not None:
                    q.wait_ge(c_dnmm.sem, need)
                q.dma_start(out=dnb[slot][:], in_=dnw[e][hcx, :, :]).then_inc(c_dnld[s % 2].sem, 16)
            em("gpsimd", _d)
            dn_ld_val[s] = (s % 2, c_dnld[s % 2].inc(16))
            return s

        wv_ld_val = {}
        wv_s_idx = [0]
        def emit_wv_dma(which, half_idx):
            s = wv_s_idx[0]; wv_s_idx[0] += 1
            src = wvt if which == 0 else wot
            need = s // 2 if s >= 2 else None
            def _d(q, src=src, half_idx=half_idx, slot=s % 2, need=need):
                if need is not None:
                    q.wait_ge(c_wvmm.sem, need)
                q.dma_start(out=wvb[slot][:], in_=src[half_idx, :, :]).then_inc(c_wvld[s % 2].sem, 16)
            em("gpsimd", _d)
            wv_ld_val[s] = (s % 2, c_wvld[s % 2].inc(16))
            return s

        def emit_stats_evac():
            stTall = c_stT.n
            def _cv(q, w=stTall):
                q.wait_ge(c_stT.sem, w)
                q.mul(st_corr[:, :], pb[7][:, 0:1], -1.0 / SEQ).then_inc(c_sevac.sem, 1)
            em("scalar", _cv)
            cv = c_sevac.inc()
            for nt in range(4):
                def _ev(q, nt=nt, w=stTall, cv=cv, lt=LOGIT["ltr_tp"]):
                    q.wait_ge(c_stT.sem, w)
                    q.wait_ge(c_sevac.sem, cv)
                    q.wait_ge(c_tp.sem, lt)
                    q.activation(otile[0][32 * nt:32 * nt + 1, 0:512],
                                 pb[6][32 * nt:32 * nt + 1, 0:512],
                                 AF.Identity, scale=1.0 / SEQ,
                                 bias=st_corr[32 * nt:32 * nt + 1, 0:1]).then_inc(c_sevac.sem, 1)
                em("scalar", _ev)
                c_sevac.inc()
            return c_sevac.n

        def emit_xbar():
            sev = c_sevac.n
            prev_ev = [None]
            for kc in range(KC):
                nt = kc // 4
                col = (kc % 4) * 128
                base = 32 * nt
                pe = prev_ev[0]
                def _tr(q, base=base, col=col, w=sev, pe=pe):
                    q.wait_ge(c_sevac.sem, w)
                    if pe is not None:
                        q.wait_ge(c_str.sem, pe)
                    q.transpose(pb[6][:, 1:2],
                                otile[0][base:base + 1, col:col + 128],
                                s_eye[base:base + 1, base:base + 1],
                                tile_position=(base, 0)).then_inc(c_str.sem, 1)
                em("tensor", _tr)
                tv = c_str.inc()
                def _ev(q, tv=tv, kc=kc):
                    q.wait_ge(c_str.sem, tv)
                    q.copy(xbar[:, kc:kc + 1],
                           pb[6][:, 1:2]).then_inc(c_str.sem, 1)
                em("scalar", _ev)
                prev_ev[0] = c_str.inc()
            return c_str.n

        def emit_vblock(which, kc, xbar_ready, vbar_ready):
            s0 = emit_wv_dma(which, 2 * kc)
            s1 = emit_wv_dma(which, 2 * kc + 1)
            def _mm(q, kc=kc, s0=s0, s1=s1, which=which, xr=xbar_ready, vr=vbar_ready):
                q.wait_ge(c_wvld[wv_ld_val[s0][0]].sem, wv_ld_val[s0][1])
                q.wait_ge(c_wvld[wv_ld_val[s1][0]].sem, wv_ld_val[s1][1])
                if xr is not None:
                    q.wait_ge(c_str.sem, xr)
                if vr is not None:
                    q.wait_ge(c_vch.sem, vr)
                mov = xbar if which == 0 else vbar
                for oc in range(16):
                    slot = (s0 + (oc // 8)) % 2
                    inst = q.matmul(pb[7][:, oc:oc + 1],
                                    wvb[slot][:, (oc % 8) * 128:(oc % 8 + 1) * 128],
                                    mov[:, kc:kc + 1],
                                    start=(kc == 0 and oc == 0),
                                    stop=(kc == KC - 1 and oc == 15),
                                    skip_group_check=True)
                inst.then_inc(c_wvmm.sem, 1)
            em("tensor", _mm)
            c_wvmm.inc()

        def emit_vevac(which):
            wall = c_wvmm.n
            for kc in range(KC):
                def _ev(q, which=which, kc=kc, w=wall):
                    q.wait_ge(c_wvmm.sem, w)
                    bias = s_bv if which == 0 else s_bo
                    dst = vbar if which == 0 else abar
                    q.activation(dst[:, kc:kc + 1],
                                 pb[7][:, kc:kc + 1], AF.Identity,
                                 bias=bias[:, kc:kc + 1]).then_inc(c_vch.sem, 1)
                em("scalar", _ev)
                c_vch.inc()
            return c_vch.n

        stat_dma_ptr = [NOWN]
        stat_chain_ptr = [NOWN]
        stat_mm_ptr = [NOWN]
        wvp = {"xbar": None, "vmm": 0, "vevac": None, "amm": 0, "aevac": None}

        def stats_slot(pair_parity):
            if pair_parity == 0 and stat_chain_ptr[0] < NCH:
                g = stat_chain_ptr[0]; stat_chain_ptr[0] += 2
                while stat_dma_ptr[0] < min(NCH, g + 4):
                    emit_stats_dma(stat_dma_ptr[0]); stat_dma_ptr[0] += 1
                r0, r1 = emit_stats_group(g)
                xn_ready[g] = r0
                xn_ready[g + 1] = r1
            lim = stat_chain_ptr[0] - (2 if stat_chain_ptr[0] < NCH else 0)
            if stat_mm_ptr[0] < lim:
                emit_stats_mms(stat_mm_ptr[0])
                emit_stats_mms(stat_mm_ptr[0] + 1)
                stat_mm_ptr[0] += 2

        def stats_done():
            return stat_mm_ptr[0] >= NCH

        def vbar_slot():
            if wvp["xbar"] is None:
                emit_stats_evac()
                wvp["xbar"] = emit_xbar()
            elif wvp["vmm"] < KC:
                emit_vblock(0, wvp["vmm"], wvp["xbar"] if wvp["vmm"] == 0 else None, None)
                wvp["vmm"] += 1
            elif wvp["vevac"] is None:
                wvp["vevac"] = emit_vevac(0)
            elif wvp["amm"] < KC:
                emit_vblock(1, wvp["amm"], None, wvp["vevac"])
                wvp["amm"] += 1
            elif wvp["aevac"] is None:
                wvp["aevac"] = emit_vevac(1)

        half_idx = [0]
        iter9_hc_free = {}
        deferred_m2 = []
        prev_m1 = [None]
        NW = NIT * FC * 2
        def emit_gu_chunk_dma(w):
            if w >= NW:
                return
            pidx, j = divmod(w, 2)
            ee, ff = divmod(pidx, FC)
            emit_gu_dma(ee, ff, j)

        for e in range(NIT):
            abuf = act[e % 2]
            for f in range(FC):
                pslot = 2 * (e * FC + f)
                if not (f == 0 and e > 0):
                    emit_gu_chunk_dma(pslot)
                    emit_gu_chunk_dma(pslot + 1)
                if 3 <= f <= 4:
                    emit_dn_dma(e, f - 3)
                pair_info = []
                for j in range(2):
                    w = half_idx[0]; half_idx[0] += 1
                    si, wv = acquire_set(2)
                    pair_info.append((si, wv))
                    def _g(q, e=e, w=w, si=si, wv=wv, chase=(e == 0)):
                        q.wait_ge(c_guld[gu_ld_val[w][0]].sem, gu_ld_val[w][1])
                        q.wait_ge(setc[si].sem, wv)
                        if e == 0 and not chase:
                            q.wait_ge(c_xnT.sem, KC)
                        slot = w % 3
                        b0, b1 = SETS[si]
                        for kc in range(KC):
                            if chase:
                                q.wait_ge(c_xnT.sem, kc + 1)
                            stat = gub[slot][:, kc * 128:(kc + 1) * 128]
                            q.matmul(pb[b0][:, 0:512], stat, xnT[:, kc, 0:512],
                                     start=(kc == 0), stop=(kc == KC - 1),
                                     skip_group_check=True)
                            inst = q.matmul(pb[b1][:, 0:512], stat, xnT[:, kc, 512:1024],
                                            start=(kc == 0), stop=(kc == KC - 1),
                                            skip_group_check=True)
                        inst.then_inc(c_gu.sem, 1)
                    em("tensor", _g)
                    c_gu.inc()
                gdone = c_gu.n
                (siA, wvA), (siB, wvB) = pair_info
                pm = prev_m1[0]
                for nt in range(NT):
                    bA = SETS[siA][nt]
                    def _si(q, bA=bA, nt=nt, w=gdone - 1, siA=siA, pm=pm):
                        q.wait_ge(c_gu.sem, w)
                        if pm is not None:
                            q.wait_ge(setc[pm[0]].sem, pm[1] + nt + 1)
                        q.activation(silu_s[:, nt * 512:(nt + 1) * 512],
                                     pb[bA][:, 0:512], AF.Silu).then_inc(setc[siA].sem, 1)
                    em("scalar", _si)
                for nt in range(NT):
                    bB = SETS[siB][nt]
                    wA = wvA + nt + 1
                    def _m1(q, bB=bB, nt=nt, f=f, abuf=abuf, siA=siA, siB=siB,
                            wA=wA, w=gdone,
                            ow=(NOWN if (e <= 1 and f == 0 and nt == 0) else None),
                            dprev=(16 * (e - 1) if (e >= 2 and f == 0 and nt == 0) else None)):
                        q.wait_ge(setc[siA].sem, wA)
                        q.wait_ge(c_gu.sem, w)
                        if ow is not None:
                            q.wait_ge(c_stT.sem, ow)
                        if dprev is not None:
                            q.wait_ge(c_dnmm.sem, dprev)
                        q.tensor_tensor(abuf[:, f, nt * 512:(nt + 1) * 512],
                                        silu_s[:, nt * 512:(nt + 1) * 512],
                                        pb[bB][:, 0:512], ALU.mult).then_inc(setc[siB].sem, 1)
                    em("vector", _m1)
                    wB = wvB + nt + 1
                    if e < NE:
                        def _m2(q, e=e, f=f, nt=nt, abuf=abuf, siB=siB, wB=wB):
                            q.wait_ge(setc[siB].sem, wB)
                            for csi, cvv in cbc_ready[e]:
                                q.wait_ge(setc[csi].sem, cvv)
                            q.tensor_tensor(abuf[:, f, nt * 512:(nt + 1) * 512],
                                            abuf[:, f, nt * 512:(nt + 1) * 512],
                                            cbc[e % 2][:, nt * 512:(nt + 1) * 512],
                                            ALU.mult).then_inc(c_act.sem, 1)
                        if e == 0:
                            deferred_m2.append(_m2)
                        else:
                            em("vector", _m2)
                    else:
                        def _m2(q, siB=siB, wB=wB):
                            q.wait_ge(setc[siB].sem, wB)
                            q.engine_nop().then_inc(c_act.sem, 1)
                        em("vector", _m2)
                    c_act.inc()
                prev_m1[0] = (siB, wvB)
                if e == 0:
                    if f in (9, 10):
                        for ee in range(4 * (f - 9), 4 * (f - 8)):
                            emit_comb_transposes(ee)
                            COMBT_W_BOX[ee] = tuple(set_expected[i] for i in range(3))
                if e == 0:
                    pass
                elif not stats_done():
                    stats_slot(0 if (f % 2 == 0 and f <= 8) else 1)
                else:
                    vbar_slot()
            if e == 0:
                cbc_ready[0] = emit_cbc(0)
                cbc_ready[1] = emit_cbc(1)
                for fn2 in deferred_m2:
                    em("vector", fn2)
                deferred_m2.clear()
            elif e + 1 < NE and (e + 1) not in cbc_ready:
                cbc_ready[e + 1] = emit_cbc(e + 1)
            emit_gu_chunk_dma(2 * (e + 1) * FC)
            emit_gu_chunk_dma(2 * (e + 1) * FC + 1)
            ACT_ALL = c_act.n
            for hc in range(HCN):
                if hc < 14:
                    emit_dn_dma(e, hc + 2)
                s = e * HCN + hc
                if True:
                    si, wv = acquire_set(2)
                    def _dn(q, s=s, si=si, wv=wv, aw=ACT_ALL, abuf=abuf, hc0=(hc == 0),
                            e=e, xw=(c_xin.n if e == 0 else None)):
                        if xw is not None:
                            q.wait_ge(c_xin.sem, xw)
                        q.wait_ge(c_dnld[dn_ld_val[s][0]].sem, dn_ld_val[s][1])
                        if not hc0:
                            q.wait_ge(c_act.sem, aw)
                        q.wait_ge(setc[si].sem, wv)
                        slot = s % 2
                        b0, b1 = SETS[si]
                        for fc in range(FC):
                            if hc0:
                                q.wait_ge(c_act.sem, 22 * e + 2 * (fc + 1))
                            stat = dnb[slot][:, fc * 128:(fc + 1) * 128]
                            q.matmul(pb[b0][:, 0:512], stat, abuf[:, fc, 0:512],
                                     start=(fc == 0), stop=(fc == FC - 1),
                                     skip_group_check=True)
                            inst = q.matmul(pb[b1][:, 0:512], stat, abuf[:, fc, 512:1024],
                                            start=(fc == 0), stop=(fc == FC - 1),
                                            skip_group_check=True)
                        inst.then_inc(c_dnmm.sem, 1)
                    em("tensor", _dn)
                    dmv = c_dnmm.inc()
                    for nt in range(NT):
                        b = SETS[si][nt]
                        def _ye(q, b=b, hc=hc, nt=nt, w=dmv, si=si,
                                lg=(LGT_ALL if e == 0 else None)):
                            q.wait_ge(c_dnmm.sem, w)
                            if lg is not None:
                                q.wait_ge(c_lgt.sem, lg)
                            q.tensor_tensor(y[:, hc, nt * 512:(nt + 1) * 512],
                                            y[:, hc, nt * 512:(nt + 1) * 512],
                                            pb[b][:, 0:512], ALU.add).then_inc(setc[si].sem, 1)
                        em("vector", _ye)
                    if e == NIT - 1:
                        iter9_hc_free[hc] = (si, set_expected[si])

        while not stats_done():
            stats_slot(0)
        while wvp["aevac"] is None:
            vbar_slot()
        ABAR = wvp["aevac"]
        XBAR_CHAIN = wvp["xbar"]

        for hc in range(HCN):
            si, freev = iter9_hc_free[hc]
            def _fa(q, hc=hc, si=si, freev=freev, aw=ABAR):
                q.wait_ge(setc[si].sem, freev)
                q.wait_ge(c_vch.sem, aw)
                q.activation(y[:, hc, :], y[:, hc, :], AF.Identity,
                             bias=abar[:, hc:hc + 1]).then_inc(c_fin.sem, 1)
            em("scalar", _fa)
            fv = c_fin.inc()
            def _od(q, hc=hc, slot=hc % 2, w=fv):
                q.wait_ge(c_fin.sem, w)
                q.dma_start(out=y_out[:, hc, :],
                            in_=y[:, hc, :]).then_inc(c_out[slot].sem, 16)
            em("sync", _od)
            c_out[hc % 2].inc(16)
        def _fini(q, w0=None, w1=None):
            q.wait_ge(c_out[0].sem, c_out[0].n)
            q.wait_ge(c_out[1].sem, c_out[1].n)
        em("sync", _fini)

        with nc.Block() as block:
            @block.sync
            def _(q):
                for fn in streams["sync"]:
                    fn(q)
            @block.gpsimd
            def _(q):
                for fn in streams["gpsimd"]:
                    fn(q)
            @block.tensor
            def _(q):
                for fn in streams["tensor"]:
                    fn(q)
            @block.scalar
            def _(q):
                for fn in streams["scalar"]:
                    fn(q)
            @block.vector
            def _(q):
                for fn in streams["vector"]:
                    fn(q)
    return nc


def _layouts(inputs):
    f32 = np.float32
    x = np.asarray(inputs["L0_attn_x"], f32)
    shared = {}
    for e in range(NE):
        A = np.asarray(inputs["L0_ffn_gate_up_weights"][e], f32)
        T = np.ascontiguousarray(A.T)
        C = T.reshape(KC, 128, 2 * FC, 128).transpose(2, 1, 0, 3)
        shared[f"guw{e}"] = np.ascontiguousarray(C).reshape(2 * FC, 128, KC * 128)
    Asg = np.asarray(inputs["L0_ffn_shared_gate_up"], f32)
    Tsg = np.ascontiguousarray(Asg.T)
    Csg = Tsg.reshape(KC, 128, 4 * FC, 128).transpose(2, 1, 0, 3)
    shared["sguw"] = np.ascontiguousarray(Csg).reshape(4 * FC, 128, KC * 128)
    for e in range(NE):
        Dw = np.asarray(inputs["L0_ffn_down_weights"][e], f32)
        Td = np.ascontiguousarray(Dw.T)
        G = Td.reshape(FC, 128, 16, 128).transpose(2, 1, 0, 3)
        shared[f"dnw{e}"] = np.ascontiguousarray(G).reshape(16, 128, FC * 128)
    Sd = np.asarray(inputs["L0_ffn_shared_down"], f32)
    Tsd = np.ascontiguousarray(Sd.T)
    for k, half in ((8, Tsd[:1408]), (9, Tsd[1408:])):
        G = np.ascontiguousarray(half).reshape(FC, 128, 16, 128).transpose(2, 1, 0, 3)
        shared[f"dnw{k}"] = np.ascontiguousarray(G).reshape(16, 128, FC * 128)
    Wv = np.asarray(inputs["L0_attn_Wv"], f32)
    Wo = np.asarray(inputs["L0_attn_Wo"], f32)
    shared["wvt"] = np.ascontiguousarray(
        np.ascontiguousarray(Wv.T).reshape(KC, 128, 2, 1024).transpose(0, 2, 1, 3)
    ).reshape(KC * 2, 128, 1024)
    shared["wot"] = np.ascontiguousarray(
        np.ascontiguousarray(Wo.T).reshape(KC, 128, 2, 1024).transpose(0, 2, 1, 3)
    ).reshape(KC * 2, 128, 1024)
    gw = np.asarray(inputs["L0_ffn_gate_weight"], f32)
    shared["gwt"] = np.ascontiguousarray(gw.T.reshape(KC, 128, NE).transpose(1, 0, 2)
                                         ).reshape(128, KC * NE)
    shared["bvv"] = np.ascontiguousarray(np.asarray(inputs["L0_attn_bv"], f32)
                                         .reshape(KC, 128).T)
    shared["bov"] = np.ascontiguousarray(np.asarray(inputs["L0_attn_bo"], f32)
                                         .reshape(KC, 128).T)
    shared["eyec"] = np.eye(128, dtype=f32)
    shared["onesc"] = np.ones((128, 1), f32)

    in_maps = []
    for c in range(NCORES):
        bb = c // 4
        sl0 = slice((c % 4) * TPC, (c % 4 + 1) * TPC)
        X = x[bb, sl0]
        xo = np.ascontiguousarray(X.T.reshape(KC, 128, TPC).transpose(1, 0, 2))
        own = [(c % 4) * 8 + i for i in range(8)]
        rest = [g for g in range(NCH) if g not in own]
        order = own + rest
        xs = np.ascontiguousarray(x[bb].reshape(NCH, 128, D)[order])
        m = dict(shared)
        m["x_own"] = xo
        m["x_stats"] = xs
        in_maps.append(m)
    return in_maps


def kernel(**inputs):
    in_maps = _layouts(inputs)
    nc = build_program()
    res = run_bass_kernel_spmd(nc, in_maps, core_ids=list(range(NCORES)))
    out = np.empty((NB, SEQ, D), np.float32)
    for c in range(NCORES):
        yT = np.asarray(res.results[c]["y_out"])
        Y = yT.transpose(1, 0, 2).reshape(D, TPC)
        bb = c // 4
        sl0 = slice((c % 4) * TPC, (c % 4 + 1) * TPC)
        out[bb, sl0] = Y.T
    return out
